# version 24
# baseline (speedup 1.0000x reference)
import numpy as np
from contextlib import ExitStack

import concourse.bass as bass
import concourse.mybir as mybir
from concourse.bass_utils import run_bass_kernel_spmd

F32 = mybir.dt.float32
F32R = mybir.dt.float32r
BF16 = mybir.dt.bfloat16
FP8 = mybir.dt.float8e4
U8 = mybir.dt.uint8
AF = mybir.ActivationFunctionType
ALU = mybir.AluOpType
AX = mybir.AxisListType

D = 2048
DC = 16
S = 4096
NT = 8
NH = 16
KVH = 4
DH = 128
TOK = 2048
NQC = 4
II = 6144
NICG = 48
ICH = 24
FC = 16
SCALE = DH ** -0.5
C_OFF = 44.0
NEG = -1.0e9

ENGS = ("sync", "gpsimd", "tensor", "vector", "scalar")


def _build():
    nc = bass.Bass("TRN2", target_bir_lowering=False)

    xT = nc.dram_tensor("xT", [D, S], F32, kind="ExternalInput").ap()
    xq = nc.dram_tensor("xq", [D, TOK], F32, kind="ExternalInput").ap()
    wq = nc.dram_tensor("wq", [D, D], F32, kind="ExternalInput").ap()
    wk = nc.dram_tensor("wk", [D, 512], F32, kind="ExternalInput").ap()
    wv = nc.dram_tensor("wv", [D, 512], F32, kind="ExternalInput").ap()
    wo = nc.dram_tensor("wo", [D, D], F32, kind="ExternalInput").ap()
    gw = nc.dram_tensor("gw", [D, II], F32, kind="ExternalInput").ap()
    uw = nc.dram_tensor("uw", [D, II], F32, kind="ExternalInput").ap()
    dw = nc.dram_tensor("dw", [II, D], F32, kind="ExternalInput").ap()
    qg = nc.dram_tensor("qg", [128, 1], F32, kind="ExternalInput").ap()
    kg = nc.dram_tensor("kg", [128, 1], F32, kind="ExternalInput").ap()
    gs = nc.dram_tensor("gs", [128, NICG], F32, kind="ExternalInput").ap()
    us = nc.dram_tensor("us", [128, NICG], F32, kind="ExternalInput").ap()
    dsc = nc.dram_tensor("dsc", [128, FC], F32, kind="ExternalInput").ap()
    mask8 = nc.dram_tensor("mask8", [8, 128, 512], F32, kind="ExternalInput").ap()
    out = nc.dram_tensor("out", [D, TOK], F32, kind="ExternalOutput").ap()
    x1s = nc.dram_tensor("x1s", [FC, 128, TOK], F32, kind="Internal").ap()
    rbc = nc.dram_tensor("rbc", [512, 512], F32, kind="Internal").ap()

    from collections import defaultdict
    cnt = defaultdict(int)
    sem = {}
    prog = {e: [] for e in ENGS}
    slot_ctr = [0]

    def new_slot():
        s = slot_ctr[0]
        slot_ctr[0] += 1
        assert s < 512
        return s

    def emit(e, fn):
        prog[e].append(fn)

    def wait(e, c, v):
        assert v <= cnt[c], f"wait on future: {e} {c} {v} > {cnt[c]}"
        if v <= 0:
            return
        emit(e, lambda eng, s=c, v=v: eng.wait_ge(sem[s], v))

    def gwait(e, g):
        if not g:
            return
        if isinstance(g, list):
            for gg in g:
                wait(e, gg[0], gg[1])
        else:
            wait(e, g[0], g[1])

    def dma(e, o, i, c):
        if e == "scalar" and cnt["S"] > 0:
            emit("scalar", lambda eng, v=cnt["S"]: eng.wait_ge(sem["S"], v))
        cnt[c] += 16
        emit(e, lambda eng, o=o, i=i, s=c: eng.dma_start(out=o, in_=i).then_inc(sem[s], 16))
        return cnt[c]

    def dma_nc(e, o, i, c):
        cnt[c] += 16

        def f(eng, o=o, i=i, sn=c):
            with nc.allow_non_contiguous_dma(reason="tiny one-time transpose"):
                return eng.dma_start(out=o, in_=i).then_inc(sem[sn], 16)
        emit(e, f)
        return cnt[c]

    def mm(o, l, r, st, sp, inc=False):
        if inc:
            cnt["T"] += 1
            emit("tensor", lambda eng, o=o, l=l, r=r, st=st, sp=sp:
                 eng.matmul(o, l, r, start=st, stop=sp).then_inc(sem["T"], 1))
            return cnt["T"]
        emit("tensor", lambda eng, o=o, l=l, r=r, st=st, sp=sp:
             eng.matmul(o, l, r, start=st, stop=sp))

    def act(o, i, f, bias=None, scale=None, inc=False):
        kw = {}
        if bias is not None:
            kw["bias"] = bias
        if scale is not None:
            kw["scale"] = scale
        if cnt["S"] > 0:
            emit("scalar", lambda eng, v=cnt["S"]: eng.wait_ge(sem["S"], v))
        cnt["S"] += 1
        emit("scalar", lambda eng, o=o, i=i, f=f, kw=kw:
             eng.activation(out=o, in_=i, func=f, **kw).then_inc(sem["S"], 1))
        return cnt["S"]

    def vop(name, *args, inc=False, **kw):
        if cnt["V"] > 0:
            emit("vector", lambda eng, v=cnt["V"]: eng.wait_ge(sem["V"], v))
        cnt["V"] += 1
        emit("vector", lambda eng, n=name, a=args, kw=kw:
             getattr(eng, n)(*a, **kw).then_inc(sem["V"], 1))
        return cnt["V"]

    OWN = {
        "sync": ["PRE", "X", 'X0', 'X1', 'X2', 'X3', 'X4', 'X5', 'X6', 'X7', 'X8', 'X9', 'X10', 'X11', 'X12', 'X13', 'X14', 'X15',
                 "SP0", "SP1", "XF0", "XF1", "XG0", "XG1", "OUT0", "OUT1",
                 "RW", "BI", "BR0", "BR1", "BB0", "BB1", "RL0", "RL1", "I20", "I21",
                 "XF2", "XF3"],
        "gpsimd": ["M", "WQ0", "WQ1", "WO0", "WO1", "GU0", "GU1", "DW0", "DW1"],
        "scalar": [],
    }

    def tinc():
        cnt["T"] += 1
        last = prog["tensor"].pop()
        emit("tensor", lambda eng, f=last: f(eng).then_inc(sem["T"], 1))
        return cnt["T"]

    ALLD = [sn for ss in (("PRE", "X", "SP0", "SP1", "XF0", "XF1", "XG0", "XG1",
                           "OUT0", "OUT1", "M", "WQ0", "WQ1", "WO0", "WO1", "GU0",
                           "GU1", "DW0", "DW1", "RW", "BI", "BR0", "BR1", "BB0",
                           "BB1", "RL0", "RL1", "I20", "I21"),) for sn in ss]

    def barrier():
        for e in ENGS:
            for sn in ("T", "V", "S"):
                wait(e, sn, cnt[sn])
            for sn in ALLD:
                wait(e, sn, cnt[sn])

    def flush(blk):
        for e in ENGS:
            fns = prog[e]
            if not fns:
                continue
            prog[e] = []
            getattr(blk, e)(lambda eng, fns=fns: [f(eng) for f in fns])

    with ExitStack() as TOP:
        for s in "PRE,M,X,X0,X1,X2,X3,X4,X5,X6,X7,X8,X9,X10,X11,X12,X13,X14,X15,RW,BI,BR0,BR1,BB0,BB1,RL0,RL1,SP0,SP1,WQ0,WQ1,WO0,WO1,I20,I21,XF0,XF1,XF2,XF3,GU0,GU1,DW0,DW1,XG0,XG1,OUT0,OUT1,T,V,S,P".split(","):
            sem[s] = TOP.enter_context(nc.semaphore(s))

        qg_sb = TOP.enter_context(nc.sbuf_tensor("qg_sb", [128, 1], F32))
        kg_sb = TOP.enter_context(nc.sbuf_tensor("kg_sb", [128, 1], F32))
        gs_sb = TOP.enter_context(nc.sbuf_tensor("gs_sb", [128, NICG], F32))
        us_sb = TOP.enter_context(nc.sbuf_tensor("us_sb", [128, NICG], F32))
        ds_sb = TOP.enter_context(nc.sbuf_tensor("ds_sb", [128, FC], F32))
        ones1 = TOP.enter_context(nc.sbuf_tensor("ones1", [128, 1], BF16))
        ones_r = TOP.enter_context(nc.sbuf_tensor("ones_r", [1, 128], F32R))
        rlrow = TOP.enter_context(nc.sbuf_tensor("rlrow", [1, 512], F32R))
        rrow2 = TOP.enter_context(nc.sbuf_tensor("rrow2", [1, 512], F32R))
        statk = TOP.enter_context(nc.sbuf_tensor("statk", [128, 2], BF16))
        statq = TOP.enter_context(nc.sbuf_tensor("statq", [128, 2], BF16))
        rowb = TOP.enter_context(nc.sbuf_tensor("rowb", [1, 7 * 512], F32))
        epsb = TOP.enter_context(nc.sbuf_tensor("epsb", [1, 4], F32))
        mkrow = TOP.enter_context(nc.sbuf_tensor("mkrow", [1, 16], F32))

        def row(i):
            return rowb[0:1, i * 512:(i + 1) * 512]

        ps = [TOP.enter_context(nc.psum_tensor(f"ps{i}", [128, 512], F32))
              for i in range(4)]
        lacc = TOP.enter_context(nc.psum_tensor("lacc", [128, 1024], F32))
        ps.append(lacc[:, 0:512])
        ps.append(lacc[:, 512:1024])
        ps.extend(TOP.enter_context(nc.psum_tensor(f"ps{i}", [128, 512], F32))
                  for i in range(6, 8))
        PS_S0, PS_S1, PS_C, PS_D, PS_L0, PS_L1, PS_G, PS_H = range(8)

        blk = TOP.enter_context(nc.Block())

        dma("sync", qg_sb[:, :], qg, "PRE")
        dma("sync", kg_sb[:, :], kg, "PRE")
        dma("sync", gs_sb[:, :], gs, "PRE")
        dma("sync", us_sb[:, :], us, "PRE")
        dma("sync", ds_sb[:, :], dsc, "PRE")
        pre_d = cnt["PRE"]
        wait("vector", "PRE", pre_d)
        vop("memset", ones1[:, :], 1.0)
        vop("memset", statk[:, 0:1], 1.0)
        vop("memset", statq[:, 0:1], 1.0)
        vop("tensor_tensor", statk[:, 1:2], kg_sb[:, :], kg_sb[:, :], ALU.mult)
        vop("tensor_tensor", statq[:, 1:2], qg_sb[:, :], qg_sb[:, :], ALU.mult)
        vop("memset", rowb[0:1, 6 * 512:6 * 512 + 128], 1.0)
        vop("tensor_copy", ones_r[:, :], rowb[0:1, 6 * 512:6 * 512 + 128])
        vop("memset", epsb[0:1, 0:1], 1e-5)
        vop("memset", epsb[0:1, 1:2], 1e-8)
        vop("memset", epsb[0:1, 2:3], 1e-35)
        vop("memset", mkrow[:, :], 0.0, inc=True)
        pre_v = cnt["V"]
        wait("scalar", "PRE", pre_d)

        with ExitStack() as SA:
            khat = SA.enter_context(nc.sbuf_tensor("khat", [128, KVH * S], BF16))
            vtok = SA.enter_context(nc.sbuf_tensor("vtok", [128, 32 * 512], BF16))
            mask_sb = SA.enter_context(nc.sbuf_tensor("mask_sb", [128, 8 * 512], BF16))
            xtt = SA.enter_context(nc.sbuf_tensor("xtt", [128, DC * 512], F32))
            xnn = SA.enter_context(nc.sbuf_tensor("xnn", [128, DC * 512], BF16))
            invB = SA.enter_context(nc.sbuf_tensor("invB", [128, 512], F32))
            minvB = SA.enter_context(nc.sbuf_tensor("minvB", [128, 512], F32))
            rall = SA.enter_context(nc.sbuf_tensor("rall", [128, KVH * 32], F32))
            mkacc = SA.enter_context(nc.sbuf_tensor("mkacc", [128, KVH], F32))
            eps128 = SA.enter_context(nc.sbuf_tensor("eps128", [128, 2], F32))
            sc4 = SA.enter_context(nc.sbuf_tensor("sc4", [128, 4], F32))
            r2t = SA.enter_context(nc.sbuf_tensor("r2t", [128, 4], F32))
            nbt = SA.enter_context(nc.sbuf_tensor("nbt", [128, 4], F32))
            mkt = SA.enter_context(nc.sbuf_tensor("mkt", [4, 128], F32))
            mk_dram = nc.dram_tensor("mk_dram", [128, 4], F32, kind="Internal").ap()
            mk_dram2 = nc.dram_tensor("mk_dram2", [1, 4], F32, kind="Internal").ap()

            dma("gpsimd", mask_sb[:, :], mask8.rearrange("m p t -> p m t"), "M")

            gd = {}

            with ExitStack() as SA1:
                wk_sb = SA1.enter_context(nc.sbuf_tensor("wk_sb", [128, DC * 512], BF16))
                wv_sb = SA1.enter_context(nc.sbuf_tensor("wv_sb", [128, DC * 512], BF16))
                ktmp = SA1.enter_context(nc.sbuf_tensor("ktmp", [128, 512], BF16))
                ksq = SA1.enter_context(nc.sbuf_tensor("ksq", [128, 512], BF16))
                xtmp = SA1.enter_context(nc.sbuf_tensor("xtmp", [128, 512], F32))
                rB = [SA1.enter_context(nc.sbuf_tensor(f"rB{i}", [128, 512], F32))
                      for i in range(2)]
                xnn2 = SA1.enter_context(nc.sbuf_tensor("xnn2", [128, DC * 512], BF16))
                xbuf = [xnn, xnn2]

                dma("gpsimd", wk_sb[:, :], wk.rearrange("(dc p) f -> p dc f", p=128), "M")
                dma("gpsimd", wv_sb[:, :], wv.rearrange("(dc p) f -> p dc f", p=128), "M")
                wkv_d = cnt["M"]

                R_MEAN, R_VAR, R_INV, R_LN, R_KR2, R_NBK, R_T1 = range(7)

                wait("tensor", "V", pre_v)
                vop("memset", eps128[:, 0:1], 1e-8)
                vop("memset", eps128[:, 1:2], -2.426015131959809)
                vop("memset", mkacc[:, :], 0.0, inc=True)
                def p1_rows(t_sumsq):
                    wait("vector", "T", t_sumsq)
                    gwait("vector", gd.get("rows1"))
                    vop("tensor_scalar", row(R_MEAN), ps[PS_L0][0:1, :], 1.0 / D, None, ALU.mult)
                    vop("tensor_scalar", row(R_VAR), ps[PS_L1][0:1, :], 1.0 / D, None, ALU.mult)
                    vop("tensor_tensor", row(R_T1), row(R_MEAN), row(R_MEAN), ALU.mult)
                    v_rows1 = vop("tensor_tensor", row(R_VAR), row(R_VAR), row(R_T1),
                                  ALU.subtract, inc=True)
                    gd["bank_l0"] = ("V", v_rows1)
                    gd["bank_l1"] = ("V", v_rows1)
                    wait("scalar", "V", v_rows1)
                    act(row(R_LN), row(R_VAR), AF.Ln, bias=epsb[0:1, 0:1], scale=1.0)
                    act(row(R_INV), row(R_LN), AF.Exp, scale=-0.5)
                    gwait("scalar", gd.get("rlrow_w"))
                    s_inv = act(rlrow[:, :], row(R_LN), AF.Exp, scale=-0.5, inc=True)
                    wait("vector", "S", s_inv)
                    gwait("vector", gd.get("r2w"))
                    v_minv = vop("tensor_tensor", rrow2[:, :], row(R_MEAN), row(R_INV),
                                 ALU.mult, inc=True)
                    wait("tensor", "S", s_inv)
                    wait("tensor", "V", v_minv)
                    mm(ps[PS_L0][:, :], ones_r[:, :], rlrow[:, :], True, True)
                    mm(ps[PS_L1][:, :], ones_r[:, :], rrow2[:, :], True, True)
                    t_lb = tinc()
                    gd["rlrow_w"] = ("T", t_lb)
                    gd["r2w"] = ("T", t_lb)
                    gd["rows1"] = ("T", t_lb)
                    wait("vector", "T", t_lb)
                    gwait("vector", gd.get("invB"))
                    vop("tensor_copy", invB[:, :], ps[PS_L0][:, :])
                    v_bi = vop("tensor_copy", minvB[:, :], ps[PS_L1][:, :], inc=True)
                    gd["bank_l0"] = ("V", v_bi)
                    gd["bank_l1"] = ("V", v_bi)

                xst = {}
                xls = []
                for dc in range(DC):
                    xls.append(dma("sync", xtt[:, dc * 512:(dc + 1) * 512],
                                   xT[:, 0:512].rearrange(
                                       "(dc p) t -> p dc t", p=128)[:, dc, :], f"X{dc}"))
                for t in range(NT):
                    xb = xbuf[t % 2]
                    if t == 0:
                        s_cp = []
                        for dc in range(DC):
                            wait("scalar", f"X{dc}", xls[dc])
                            s_cp.append(act(xb[:, dc * 512:(dc + 1) * 512],
                                            xtt[:, dc * 512:(dc + 1) * 512], AF.Copy))
                        tsx = []
                        for dc in range(DC):
                            wait("tensor", "S", s_cp[dc])
                            mm(ps[PS_L0][0:1, :], ones1[:, :], xb[:, dc * 512:(dc + 1) * 512],
                               dc == 0, dc == DC - 1)
                            tsx.append(tinc())
                        s_sq = []
                        for dc in range(DC):
                            wait("scalar", "T", tsx[dc])
                            s_sq.append(act(xb[:, dc * 512:(dc + 1) * 512],
                                            xb[:, dc * 512:(dc + 1) * 512], AF.Square))
                        for dc in range(DC):
                            wait("tensor", "S", s_sq[dc])
                            mm(ps[PS_L1][0:1, :], ones1[:, :], xb[:, dc * 512:(dc + 1) * 512],
                               dc == 0, dc == DC - 1)
                        p1_rows(tinc())

                    gwait("vector", gd.get("xnn"))
                    vaff = []
                    vsub = []
                    xls_next = []
                    for dc in range(DC):
                        va = vop("tensor_tensor", xtmp[:, :], xtt[:, dc * 512:(dc + 1) * 512],
                                 invB[:, :], ALU.mult)
                        vaff.append(va)
                        vsub.append(vop("tensor_tensor", xb[:, dc * 512:(dc + 1) * 512],
                                        xtmp[:, :], minvB[:, :], ALU.subtract, inc=True))
                        if t < NT - 1:
                            wait("sync", "V", va)
                            xls_next.append(dma(
                                "sync", xtt[:, dc * 512:(dc + 1) * 512],
                                xT[:, (t + 1) * 512:(t + 2) * 512].rearrange(
                                    "(dc p) t -> p dc t", p=128)[:, dc, :], f"X{dc}"))
                    v_xn = cnt["V"]
                    gd["invB"] = ("V", v_xn)

                    def mk_pstat(dc, tn):
                        xn_ = xbuf[tn % 2]
                        def fn():
                            wait("vector", f"X{dc}", xls_next[dc])
                            if dc == 0:
                                gwait("vector", gd.get("xnn_sq"))
                            v_cp = vop("tensor_copy", xn_[:, dc * 512:(dc + 1) * 512],
                                       xtt[:, dc * 512:(dc + 1) * 512], inc=True)
                            wait("tensor", "V", v_cp)
                            if dc == 0:
                                gwait("tensor", gd.get("bank_l0"))
                                gwait("tensor", gd.get("bank_l1"))
                            mm(ps[PS_L0][0:1, :], ones1[:, :],
                               xn_[:, dc * 512:(dc + 1) * 512], dc == 0, dc == DC - 1)
                            t_s = tinc()
                            wait("vector", "T", t_s)
                            v_sq = vop("tensor_tensor", xn_[:, dc * 512:(dc + 1) * 512],
                                       xn_[:, dc * 512:(dc + 1) * 512],
                                       xn_[:, dc * 512:(dc + 1) * 512], ALU.mult, inc=True)
                            wait("tensor", "V", v_sq)
                            mm(ps[PS_L1][0:1, :], ones1[:, :],
                               xn_[:, dc * 512:(dc + 1) * 512], dc == 0, dc == DC - 1)
                            xst["t_sumsq"] = tinc()
                        return fn
                    if t < NT - 1:
                        pcls = [mk_pstat(dc, t + 1) for dc in range(DC)]
                        pcls.append(lambda: p1_rows(xst.pop("t_sumsq")))
                    else:
                        pcls = []

                    KB = [PS_S0, PS_S1, PS_G, PS_H]
                    if t == 0:
                        wait("tensor", "M", wkv_d)
                    t_kp = {}
                    for kvh in range(KVH):
                        b = KB[kvh]
                        gwait("tensor", gd.get(f"kbank{kvh}"))
                        for dc in range(DC):
                            if kvh == 0:
                                wait("tensor", "V", vsub[dc])
                            mm(ps[b][:, :], wk_sb[:, dc * 512 + kvh * 128: dc * 512 + (kvh + 1) * 128],
                               xb[:, dc * 512:(dc + 1) * 512], dc == 0, dc == DC - 1)
                        t_kp[kvh] = tinc()
                        if kvh >= 1:
                            for _ in range(2):
                                if pcls:
                                    pcls.pop(0)()

                    for ts_ in range(4):
                        b = PS_C if ts_ % 2 == 0 else PS_D
                        gwait("tensor", gd.get(f"vbank{ts_ % 2}"))
                        if ts_ < 2:
                            gwait("tensor", gd.get("kstatbank"))
                        for dc in range(DC):
                            mm(ps[b][:, :],
                               xb[:, dc * 512 + ts_ * 128: dc * 512 + (ts_ + 1) * 128],
                               wv_sb[:, dc * 512:(dc + 1) * 512], dc == 0, dc == DC - 1)
                        t_vp = tinc()
                        for _ in range(2):
                            if pcls:
                                pcls.pop(0)()
                        wait("vector", "T", t_vp)
                        kt_idx = t * 4 + ts_
                        v_vc = vop("tensor_copy", vtok[:, kt_idx * 512:(kt_idx + 1) * 512],
                                   ps[b][:, :], inc=True)
                        gd[f"vbank{ts_ % 2}"] = ("V", v_vc)
                        v_vc_last = v_vc

                    for kvh in range(KVH):
                        b = KB[kvh]
                        wait("scalar", "T", t_kp[kvh])
                        gwait("scalar", gd.get("ksq"))
                        act(ktmp[:, :], ps[b][:, :], AF.Copy)
                        s_ksq = act(ksq[:, :], ktmp[:, :], AF.Square, inc=True)
                        wait("tensor", "S", s_ksq)
                        wait("tensor", "V", v_vc_last)
                        gwait("tensor", gd.get("kstatbank"))
                        for ts_ in range(4):
                            mm(ps[PS_C][:, kvh * 4 + ts_: kvh * 4 + ts_ + 1],
                               ksq[:, ts_ * 128:(ts_ + 1) * 128], ones1[:, :], True, True)
                            mm(ps[PS_D][:, kvh * 4 + ts_: kvh * 4 + ts_ + 1],
                               ksq[:, ts_ * 128:(ts_ + 1) * 128], statk[:, 1:2], True, True)
                        t_ks = tinc()
                        gd["ksq"] = ("T", t_ks)
                        wait("scalar", "T", t_ks)
                        gwait("scalar", gd.get("sc4"))
                        s_ln = act(sc4[:, :], ps[PS_C][:, kvh * 4: kvh * 4 + 4], AF.Ln,
                                   bias=eps128[:, 0:1], scale=1.0 / DH, inc=True)
                        act(rall[:, kvh * 32 + t * 4: kvh * 32 + t * 4 + 4], sc4[:, :],
                            AF.Exp, scale=-0.5, bias=eps128[:, 1:2])
                        s_kr = act(r2t[:, :], sc4[:, :], AF.Exp, scale=-1.0, inc=True)
                        gd["sc4"] = ("S", s_kr)
                        wait("vector", "S", s_kr)
                        vop("tensor_tensor", nbt[:, :], ps[PS_D][:, kvh * 4: kvh * 4 + 4],
                            r2t[:, :], ALU.mult)
                        vop("tensor_reduce", nbt[:, 0:1], nbt[:, :], AX.X, ALU.max)
                        v_nb = vop("tensor_tensor", mkacc[:, kvh:kvh + 1],
                                   mkacc[:, kvh:kvh + 1], nbt[:, 0:1], ALU.max, inc=True)
                        gd["kstatbank"] = ("V", v_nb)
                        v_kh = vop("tensor_scalar",
                                   khat[:, kvh * S + t * 512: kvh * S + (t + 1) * 512],
                                   ps[b][:, :], kg_sb[:, :], None, ALU.mult, inc=True)
                        gd[f"kbank{kvh}"] = ("V", v_kh)
                        for _ in range(2):
                            if pcls:
                                pcls.pop(0)()
                    while pcls:
                        pcls.pop(0)()
                    gd["xnn"] = ("T", cnt["T"])
                    gd["xnn_sq"] = ("T", cnt["T"])

                wait("sync", "V", cnt["V"])
                mkd = dma("sync", mk_dram, mkacc[:, :], "X")
                wait("sync", "X", mkd)
                mkd2 = dma_nc("sync", mkt[:, :], mk_dram.rearrange("p k -> k p"), "X")
                wait("vector", "X", mkd2)
                v_mk4 = vop("tensor_reduce", nbt[0:4, 0:1], mkt[:, :], AX.X,
                            ALU.max, inc=True)
                wait("sync", "V", v_mk4)
                mkd3 = dma_nc("sync", mk_dram2.rearrange("one k -> k one"), nbt[0:4, 0:1], "X")
                wait("sync", "X", mkd3)
                mkd4 = dma("sync", mkrow[0:1, 0:4], mk_dram2, "X")

                barrier()
                flush(blk)

            with ExitStack() as SA2:
                qhat = [SA2.enter_context(nc.sbuf_tensor(f"qhat{i}", [128, 1024], BF16))
                        for i in range(2)]
                attno = SA2.enter_context(nc.sbuf_tensor("attno", [128, NH * 512], BF16))
                wqhg = [SA2.enter_context(nc.sbuf_tensor(f"wqhg{i}", [128, DC * 256], BF16))
                        for i in range(2)]
                wofc = [SA2.enter_context(nc.sbuf_tensor(f"wofc{i}", [128, DC * 128], BF16))
                        for i in range(2)]
                pbt = SA2.enter_context(nc.sbuf_tensor("pbt", [128, 3072], BF16))

                def pslot(sl, hp):
                    return pbt[:, sl * 1024 + hp * 512: sl * 1024 + (hp + 1) * 512]
                qraw = [SA2.enter_context(nc.sbuf_tensor(f"qraw{i}", [128, 512], BF16))
                        for i in range(2)]
                qsq = SA2.enter_context(nc.sbuf_tensor("qsq", [128, 512], BF16))
                rBq = [SA2.enter_context(nc.sbuf_tensor(f"rBq{i}", [128, 512], BF16))
                       for i in range(2)]
                rlB = [SA2.enter_context(nc.sbuf_tensor(f"rlB{i}", [128, 512], F32))
                       for i in range(2)]
                biasB = SA2.enter_context(nc.sbuf_tensor("biasB", [128, 4], F32))
                x1t = [SA2.enter_context(nc.sbuf_tensor(f"x1t{i}", [128, 512], F32))
                       for i in range(2)]



                R_MEAN, R_VAR, R_INV, R_LN, R_QR2, R_NBQ, R_T1 = range(7)
                wait("vector", "M", 48)
                inv2_slot = {}
                minv2_slot = {}

                preload = [None]
                xst = {}
                for qc in range(NQC):
                    K_QC = 8 * qc + 8
                    if preload[0] is not None:
                        xls = preload[0]
                    else:
                        xls = []
                        for dc in range(DC):
                            xls.append(dma("sync", xtt[:, dc * 512:(dc + 1) * 512],
                                           xq[:, qc * 512:(qc + 1) * 512].rearrange(
                                               "(dc p) t -> p dc t", p=128)[:, dc, :], f"X{dc}"))
                    if "t_sumsq" in xst:
                        t_sumsq = xst.pop("t_sumsq")
                    else:
                        s_cp = []
                        for dc in range(DC):
                            wait("scalar", f"X{dc}", xls[dc])
                            if dc == 0:
                                gwait("scalar", gd.get("xnn_sq"))
                            s_cp.append(act(xnn[:, dc * 512:(dc + 1) * 512],
                                            xtt[:, dc * 512:(dc + 1) * 512], AF.Copy))
                        tsx = []
                        for dc in range(DC):
                            wait("tensor", "S", s_cp[dc])
                            if dc == 0:
                                gwait("tensor", gd.get("av10"))
                                gwait("tensor", gd.get("av11"))
                            mm(ps[PS_G][0:1, :], ones1[:, :], xnn[:, dc * 512:(dc + 1) * 512],
                               dc == 0, dc == DC - 1)
                            tsx.append(tinc())
                        s_sq = []
                        for dc in range(DC):
                            wait("scalar", "T", tsx[dc])
                            s_sq.append(act(xnn[:, dc * 512:(dc + 1) * 512],
                                            xnn[:, dc * 512:(dc + 1) * 512], AF.Square))
                        for dc in range(DC):
                            wait("tensor", "S", s_sq[dc])
                            mm(ps[PS_H][0:1, :], ones1[:, :], xnn[:, dc * 512:(dc + 1) * 512],
                               dc == 0, dc == DC - 1)
                        t_sumsq = tinc()
                    wait("vector", "T", t_sumsq)
                    gwait("vector", gd.get("rows1"))
                    vop("tensor_scalar", row(R_MEAN), ps[PS_G][0:1, :], 1.0 / D, None, ALU.mult)
                    vop("tensor_scalar", row(R_VAR), ps[PS_H][0:1, :], 1.0 / D, None, ALU.mult)
                    gwait("vector", gd.get("t1m"))
                    vop("tensor_tensor", row(R_T1), row(R_MEAN), row(R_MEAN), ALU.mult)
                    v_rows1 = vop("tensor_tensor", row(R_VAR), row(R_VAR), row(R_T1),
                                  ALU.subtract, inc=True)
                    gd["av10"] = ("V", v_rows1)
                    gd["av11"] = ("V", v_rows1)
                    wait("scalar", "V", v_rows1)
                    act(row(R_LN), row(R_VAR), AF.Ln, bias=epsb[0:1, 0:1], scale=1.0)
                    act(row(R_INV), row(R_LN), AF.Exp, scale=-0.5)
                    gwait("scalar", gd.get("rlrow_w"))
                    s_inv = act(rlrow[:, :], row(R_LN), AF.Exp, scale=-0.5, inc=True)
                    wait("vector", "S", s_inv)
                    gwait("vector", gd.get("r2w"))
                    v_minv = vop("tensor_tensor", rrow2[:, :], row(R_MEAN), row(R_INV),
                                 ALU.mult, inc=True)
                    wait("tensor", "S", s_inv)
                    wait("tensor", "V", v_minv)
                    gwait("tensor", gd.get("av00"))
                    gwait("tensor", gd.get("av01"))
                    mm(ps[PS_C][:, :], ones_r[:, :], rlrow[:, :], True, True)
                    mm(ps[PS_D][:, :], ones_r[:, :], rrow2[:, :], True, True)
                    t_lb = tinc()
                    gd["rlrow_w"] = ("T", t_lb)
                    gd["r2w"] = ("T", t_lb)
                    gd["rows1"] = ("T", t_lb)
                    wait("vector", "T", t_lb)
                    gwait("vector", gd.get("invB"))
                    vop("tensor_copy", invB[:, :], ps[PS_C][:, :])
                    v_bi = vop("tensor_copy", minvB[:, :], ps[PS_D][:, :], inc=True)
                    gd["av00"] = ("V", v_bi)
                    gd["av01"] = ("V", v_bi)
                    gwait("vector", gd.get("xnn"))
                    gwait("vector", gd.get("rlB0"))
                    vsubq = []
                    for dc in range(DC):
                        vop("tensor_tensor", rlB[0][:, :], xtt[:, dc * 512:(dc + 1) * 512],
                            invB[:, :], ALU.mult)
                        vsubq.append(vop("tensor_tensor", xnn[:, dc * 512:(dc + 1) * 512],
                                         rlB[0][:, :], minvB[:, :], ALU.subtract, inc=True))
                    v_xn = cnt["V"]
                    gd["invB"] = ("V", v_xn)
                    gd["rlB0"] = ("V", v_xn)

                    P = {}

                    def part1(hg):
                        wb_ = hg % 2
                        PAIR = [PS_C, PS_D] if hg % 2 == 0 else [PS_G, PS_H]
                        st = {"vqh": {}, "vqc": {}, "vqsq": {}, "hg": hg}
                        gwait("gpsimd", gd.get(f"wqhg{wb_}"))
                        wqd = dma("gpsimd", wqhg[wb_][:, :],
                                  wq[:, hg * 256:(hg + 1) * 256].rearrange("(dc p) f -> p dc f", p=128),
                                  f"WQ{wb_}")
                        for hp in range(2):
                            qb = PAIR[hp]
                            wait("tensor", f"WQ{wb_}", wqd)
                            if hg > 0 or hp > 0:
                                wait("tensor", "V", v_xn)
                            gwait("tensor", gd.get(f"av{hg % 2}{hp}"))
                            for dc in range(DC):
                                if hg == 0 and hp == 0:
                                    wait("tensor", "V", vsubq[dc])
                                mm(ps[qb][:, :],
                                   wqhg[wb_][:, dc * 256 + hp * 128: dc * 256 + (hp + 1) * 128],
                                   xnn[:, dc * 512:(dc + 1) * 512], dc == 0, dc == DC - 1)
                            t_qp = tinc()
                            wait("vector", "T", t_qp)
                            gwait("vector", gd.get(f"qraw{hp}"))
                            st["vqc"][hp] = vop("tensor_copy", qraw[hp][:, :], ps[qb][:, :],
                                                inc=True)
                        gd[f"wqhg{wb_}"] = ("T", cnt["T"])
                        P[hg] = st

                    def part2(hg, hp):
                        wb_ = hg % 2
                        PAIR = [PS_C, PS_D] if hg % 2 == 0 else [PS_G, PS_H]
                        st = P[hg]
                        H = hg * 2 + hp
                        kvh = H // 4
                        wait("vector", "V", st["vqc"][hp])
                        gwait("vector", gd.get("qsq"))
                        v_qsq = vop("tensor_tensor", qsq[:, :], qraw[hp][:, :],
                                    qraw[hp][:, :], ALU.mult, inc=True)
                        wait("tensor", "V", v_qsq)
                        if hp == 1:
                            wait("tensor", "S", st["sqr0"])
                            wait("tensor", "V", st["vm20"])
                            wait("tensor", "V", st["vrbq0"])
                        mm(ps[PAIR[0]][0:1, :], ones1[:, :], qsq[:, :], True, True)
                        mm(ps[PAIR[1]][0:1, :], statq[:, 1:2], qsq[:, :], True, True)
                        t_qs = tinc()
                        gd["qsq"] = ("T", t_qs)
                        wait("scalar", "T", t_qs)
                        gwait("scalar", gd.get("qrow"))
                        act(row(R_LN), ps[PAIR[0]][0:1, :], AF.Ln, bias=epsb[0:1, 1:2], scale=1.0 / DH)
                        gwait("scalar", gd.get("rlrow_w"))
                        act(rlrow[:, :], row(R_LN), AF.Exp, scale=-0.5)
                        s_qr = act(row(R_QR2), row(R_LN), AF.Exp, scale=-1.0, inc=True)
                        wait("vector", "S", s_qr)
                        vop("tensor_tensor", row(R_NBQ), ps[PAIR[1]][0:1, :], row(R_QR2), ALU.mult)
                        vop("tensor_reduce", row(R_T1)[0:1, 0:1], row(R_NBQ), AX.X, ALU.max)
                        gwait("vector", gd.get("t1m"))
                        v_m2 = vop("tensor_tensor", row(R_T1)[0:1, 1:2], row(R_T1)[0:1, 0:1],
                                   mkrow[0:1, kvh:kvh + 1], ALU.mult, inc=True)
                        wait("scalar", "V", v_m2)
                        act(row(R_T1)[0:1, 2:3], row(R_T1)[0:1, 1:2], AF.Ln,
                            bias=epsb[0:1, 2:3], scale=1.0 / DH)
                        s_mv = act(row(R_T1)[0:1, 3:4], row(R_T1)[0:1, 2:3], AF.Exp,
                                   scale=0.5, inc=True)
                        gd["t1m"] = ("S", s_mv)
                        wait("vector", "S", s_mv)
                        gwait("vector", gd.get("biasrow"))
                        v_bias = vop("tensor_scalar", mkrow[0:1, 8 + hp:9 + hp],
                                     row(R_T1)[0:1, 3:4], -1.0, C_OFF,
                                     ALU.mult, op1=ALU.add, inc=True)
                        if hp == 0:
                            st["sqr0"] = s_qr
                            st["vm20"] = v_m2
                        st[f"p2c{hp}"] = (s_qr, v_m2, v_bias)

                    def part2b(hg, hp):
                        wb_ = hg % 2
                        PAIR = [PS_C, PS_D] if hg % 2 == 0 else [PS_G, PS_H]
                        st = P[hg]
                        s_qr, v_m2, v_bias = st[f"p2c{hp}"]
                        wait("tensor", "S", s_qr)
                        wait("tensor", "V", v_m2)
                        mm(ps[PAIR[hp]][:, :], ones_r[:, :], rlrow[:, :], True, True)
                        t_qb = tinc()
                        gd["rlrow_w"] = ("T", t_qb)
                        wait("vector", "T", t_qb)
                        gwait("vector", gd.get(f"rBq{hp}"))
                        v_rbq = vop("tensor_copy", rBq[hp][:, :], ps[PAIR[hp]][:, :], inc=True)
                        gwait("vector", gd.get(f"qhat{wb_}") if hp == 0 else None)
                        v_qh = vop("scalar_tensor_tensor",
                                   qhat[wb_][:, hp * 512:(hp + 1) * 512],
                                   qraw[hp][:, :], qg_sb[:, :], rBq[hp][:, :],
                                   ALU.mult, ALU.mult, inc=True)
                        gd[f"rBq{hp}"] = ("V", v_qh)
                        gd[f"qraw{hp}"] = ("V", v_qh)
                        st["vqh"][hp] = v_qh
                        if hp == 0:
                            st["vrbq0"] = v_rbq
                        else:
                            st["statdone"] = ("V", v_rbq)
                            slb = new_slot()
                            wait("sync", "V", v_bias)
                            wbb = dma("sync", rbc[slb:slb + 1, 0:2], mkrow[0:1, 8:10], "RW")
                            wait("sync", "RW", wbb)
                            gd["biasrow"] = ("RW", wbb)
                            gwait("sync", gd.get(f"biasB{wb_}"))
                            st["bbd"] = dma("sync", biasB[:, wb_ * 2:wb_ * 2 + 2],
                                            rbc[slb:slb + 1, 0:2].broadcast_to([128, 2]),
                                            f"BB{wb_}")

                    def attention(hg, inject=(), pre=()):
                        wb_ = hg % 2
                        st = P[hg]
                        kvh = hg // 2
                        AVB = [PS_C, PS_D] if hg % 2 == 0 else [PS_G, PS_H]
                        LB = [PS_L0, PS_L1]
                        SBK = [PS_S0, PS_S1]
                        KB = 8 * qc
                        s_exp = {0: {}, 1: {}}
                        t_cmb = {0: {}, 1: {}}
                        v_mask = {0: {}, 1: {}}
                        v_pair = {0: {}, 1: {}}
                        inj = {}
                        for kk, fn in inject:
                            inj.setdefault(kk, []).append(fn)
                        prei = {}
                        for kk, fn in pre:
                            prei.setdefault(kk, []).append(fn)
                        for k in range(K_QC + 1):
                            for fn in prei.pop(k, ()):
                                fn()
                            for hp in range(2):
                                H = hg * 2 + hp
                                qv = qhat[wb_][:, hp * 512:(hp + 1) * 512]
                                if k >= 1:
                                    j = k - 1
                                    if v_mask[hp][j]:
                                        wait("tensor", "V", v_mask[hp][j])
                                    else:
                                        wait("tensor", "S", s_exp[hp][j])
                                    if k == 1:
                                        gwait("tensor", gd.get(f"av{hg % 2}{hp}"))
                                        gwait("tensor", st.get("statdone"))
                                        gwait("tensor", gd.get(f"lbank{hp}"))
                                    mm(ps[AVB[hp]][:, :],
                                       vtok[:, j * 512 + kvh * 128: j * 512 + (kvh + 1) * 128],
                                       pslot(j % 2, hp), j == 0, j == K_QC - 1)
                                    if k >= 3 and k % 2 == 1:
                                        wait("tensor", "V", v_pair[hp][k - 2])
                                        mm(ps[LB[hp]][0:1, :], ones1[:, :],
                                           pslot(2, hp), k == 3, False)
                                    if k == K_QC:
                                        wait("tensor", "V", v_pair[hp][k - 1])
                                        mm(ps[LB[hp]][0:1, :], ones1[:, :],
                                           pslot(2, hp), False, True)
                                if k < K_QC:
                                    if k == 0:
                                        gwait("tensor", gd.get(f"sbank{hp}"))
                                        wait("tensor", "V", st["vqh"][hp])
                                    mm(ps[SBK[hp]][:, :],
                                       khat[:, kvh * S + k * 128: kvh * S + (k + 1) * 128],
                                       qv, True, True)
                                t_cmb[hp][k] = tinc()
                                if k < K_QC:
                                    wait("scalar", "T", t_cmb[hp][k])
                                    if k == 0:
                                        wait("scalar", f"BB{wb_}", st["bbd"])
                                        gwait("scalar", gd.get(f"pb_all{hp}"))
                                    if k >= 2:
                                        wait("scalar", "V",
                                             v_pair[hp][k - 1 if k % 2 == 0 else k - 2])
                                    s_exp[hp][k] = act(
                                        pslot(k % 2, hp), ps[SBK[hp]][:, :], AF.Exp,
                                        bias=biasB[:, wb_ * 2 + hp: wb_ * 2 + hp + 1],
                                        scale=rall[:, kvh * 32 + k: kvh * 32 + k + 1],
                                        inc=True)
                                    if k >= KB:
                                        mi = k - KB
                                        wait("vector", "S", s_exp[hp][k])
                                        v_mask[hp][k] = vop(
                                            "tensor_tensor", pslot(k % 2, hp),
                                            pslot(k % 2, hp),
                                            mask_sb[:, mi * 512:(mi + 1) * 512],
                                            ALU.mult, inc=True)
                                    else:
                                        v_mask[hp][k] = None
                                    if k % 2 == 1:
                                        if not v_mask[hp][k]:
                                            wait("vector", "S", s_exp[hp][k])
                                        v_pair[hp][k] = vop(
                                            "tensor_tensor", pslot(2, hp),
                                            pslot(k % 2, hp),
                                            pslot((k - 1) % 2, hp), ALU.add,
                                            inc=True)
                            for fn in inj.get(k, ()):
                                fn()
                        for fns in prei.values():
                            for fn in fns:
                                fn()
                        for hp in range(2):
                            gd[f"sbank{hp}"] = ("S", s_exp[hp][K_QC - 1])
                            gd[f"pb_all{hp}"] = ("T", t_cmb[hp][K_QC])
                        wait("scalar", "T", t_cmb[0][K_QC])
                        gwait("scalar", gd.get("rlrow_w"))
                        act(row(R_LN), ps[LB[0]][0:1, :], AF.Ln, scale=1.0)
                        s_rl0 = act(rlrow[:, :], row(R_LN), AF.Exp, scale=-1.0, inc=True)

                        def mk_fin(hp, s_rl_in):
                            pr = hg % 2

                            def fn():
                                if hp == 1:
                                    wait("scalar", "T", t_cmb[1][K_QC])
                                    gwait("scalar", gd.get("rlrow_w"))
                                    act(row(R_LN), ps[LB[1]][0:1, :], AF.Ln, scale=1.0)
                                    s_rl = act(rlrow[:, :], row(R_LN), AF.Exp, scale=-1.0,
                                               inc=True)
                                else:
                                    s_rl = s_rl_in
                                wait("tensor", "S", s_rl)
                                mm(ps[LB[hp]][:, :], ones_r[:, :], rlrow[:, :], True, True)
                                t_b = tinc()
                                gd["rlrow_w"] = ("T", t_b)
                                wait("vector", "T", t_b)
                                gwait("vector", gd.get(f"rlB{hp}"))
                                v_cp = vop("tensor_copy", rlB[hp][:, :], ps[LB[hp]][:, :],
                                           inc=True)
                                gd[f"lbank{hp}"] = ("V", v_cp)
                                wait("vector", "T", t_cmb[hp][K_QC])
                                v_ev = vop("tensor_tensor",
                                           attno[:, (hg * 2 + hp) * 512:(hg * 2 + hp + 1) * 512],
                                           ps[AVB[hp]][:, :], rlB[hp][:, :], ALU.mult,
                                           inc=True)
                                gd[f"av{pr}{hp}"] = ("V", v_ev)
                                gd[f"rlB{hp}"] = ("V", v_ev)
                            return fn

                        defer = ((0, mk_fin(0, s_rl0)), (1, mk_fin(1, None)))
                        gd[f"biasB{wb_}"] = ("S", cnt["S"])
                        gd[f"qhat{wb_}"] = ("T", cnt["T"])
                        return defer

                    if K_QC <= 8:
                        ka, kb, kc, kd = 4, 6, 6, 8
                    else:
                        ka, kb, kc, kd = 5, 7, 8, 10
                    part1(0)
                    part2(0, 0)
                    part2b(0, 0)
                    part2(0, 1)
                    part2b(0, 1)
                    evd = ()
                    for hg in range(8):
                        if hg < 7:
                            evd = attention(hg,
                                            inject=((2, lambda h=hg + 1: part1(h)),
                                                    (ka, lambda h=hg + 1: part2(h, 0)),
                                                    (kb, lambda h=hg + 1: part2b(h, 0)),
                                                    (kc, lambda h=hg + 1: part2(h, 1)),
                                                    (kd, lambda h=hg + 1: part2b(h, 1))),
                                            pre=evd)
                        else:
                            evd = attention(hg, pre=evd)
                    for _k, fn in evd:
                        fn()
                    v_evall = cnt["V"]

                    v_x1b_of = {}
                    def ln2_stats(i):
                        wait("tensor", "V", v_x1b_of[i])
                        if i == 0:
                            gwait("tensor", gd.get("lbank0"))
                            gwait("tensor", gd.get("lbank1"))
                        mm(ps[PS_L0][0:1, :], ones1[:, :], qraw[0][:, :], i == 0, i == FC - 1)
                        mm(ps[PS_L1][0:1, :], ones1[:, :], qsq[:, :], i == 0, i == FC - 1)
                        t_st = tinc()
                        gd["qsq"] = ("T", t_st)
                        gd["qraw0"] = ("T", t_st)
                        return t_st

                    def mk_xstat(dc):
                        def fn():
                            wait("vector", f"X{dc}", preload[0][dc])
                            v_cp = vop("tensor_copy", xnn[:, dc * 512:(dc + 1) * 512],
                                       xtt[:, dc * 512:(dc + 1) * 512], inc=True)
                            wait("tensor", "V", v_cp)
                            if dc == 0:
                                gwait("tensor", gd.get("av10"))
                                gwait("tensor", gd.get("av11"))
                            mm(ps[PS_G][0:1, :], ones1[:, :],
                               xnn[:, dc * 512:(dc + 1) * 512], dc == 0, dc == DC - 1)
                            t_s = tinc()
                            wait("vector", "T", t_s)
                            v_sq = vop("tensor_tensor", xnn[:, dc * 512:(dc + 1) * 512],
                                       xnn[:, dc * 512:(dc + 1) * 512],
                                       xnn[:, dc * 512:(dc + 1) * 512], ALU.mult, inc=True)
                            wait("tensor", "V", v_sq)
                            mm(ps[PS_H][0:1, :], ones1[:, :],
                               xnn[:, dc * 512:(dc + 1) * 512], dc == 0, dc == DC - 1)
                            xst["t_sumsq"] = tinc()
                        return fn
                    xstat_cls = ([mk_xstat(dc) for dc in range(DC)]
                                 if qc < NQC - 1 else [])
                    for fc in range(FC):
                        wb_ = fc % 2
                        gwait("gpsimd", gd.get(f"wofc{wb_}"))
                        wod = dma("gpsimd", wofc[wb_][:, :],
                                  wo[:, fc * 128:(fc + 1) * 128].rearrange("(dc p) f -> p dc f", p=128),
                                  f"WO{wb_}")
                        b = PS_S0 if fc % 2 == 0 else PS_S1
                        wait("tensor", f"WO{wb_}", wod)
                        if fc == 0:
                            wait("tensor", "V", v_evall)
                        gwait("tensor", gd.get(f"sbank{fc % 2}"))
                        for dc in range(DC):
                            mm(ps[b][:, :], wofc[wb_][:, dc * 128:(dc + 1) * 128],
                               attno[:, dc * 512:(dc + 1) * 512], dc == 0, dc == DC - 1)
                        t_wo = tinc()
                        if fc >= 1:
                            ln2_stats(fc - 1)
                        wait("vector", "T", t_wo)
                        gwait("vector", gd.get(f"x1t{fc % 2}"))
                        if fc == 0:
                            wores = []
                            gwait("vector", gd.get("rlB0"))
                        wores.append(vop("tensor_tensor", rlB[0][:, :], ps[b][:, :],
                                         xtt[:, fc * 512:(fc + 1) * 512], ALU.add))
                        v_x1 = vop("tensor_tensor", x1t[fc % 2][:, :], rlB[0][:, :],
                                   xnn[:, fc * 512:(fc + 1) * 512], ALU.add, inc=True)
                        gd[f"sbank{fc % 2}"] = ("V", v_x1)
                        gwait("vector", gd.get("qraw0"))
                        vop("tensor_copy", qraw[0][:, :], x1t[fc % 2][:, :])
                        gwait("vector", gd.get("qsq"))
                        v_x1b = vop("tensor_tensor", qsq[:, :], qraw[0][:, :], qraw[0][:, :],
                                    ALU.mult, inc=True)
                        v_x1b_of[fc] = v_x1b
                        wait("sync", "V", v_x1)
                        spd = dma("sync", x1s[fc, :, qc * 512:(qc + 1) * 512],
                                  x1t[fc % 2][:, :], f"SP{fc % 2}")
                        gd[f"x1t{fc % 2}"] = (f"SP{fc % 2}", spd)
                        gd[f"wofc{wb_}"] = ("T", t_wo)
                        if fc == FC - 1:
                            gd["rlB0"] = ("V", v_x1)
                        if qc < NQC - 1:
                            if fc == 0:
                                preload[0] = []
                            wait("sync", "V", wores[fc])
                            preload[0].append(dma(
                                "sync", xtt[:, fc * 512:(fc + 1) * 512],
                                xq[:, (qc + 1) * 512:(qc + 2) * 512].rearrange(
                                    "(dc p) t -> p dc t", p=128)[:, fc, :], f"X{fc}"))
                        else:
                            preload[0] = None
                        if fc >= 2 and xstat_cls:
                            xstat_cls.pop(0)()
                    while xstat_cls:
                        xstat_cls.pop(0)()
                    ln2_stats(FC - 1)
                    t_ln2 = cnt["T"]
                    wait("vector", "T", t_ln2)
                    gwait("vector", gd.get("rows1"))
                    vop("tensor_scalar", row(R_MEAN), ps[PS_L0][0:1, :], 1.0 / D, None, ALU.mult)
                    vop("tensor_scalar", row(R_VAR), ps[PS_L1][0:1, :], 1.0 / D, None, ALU.mult)
                    vop("tensor_tensor", row(R_T1), row(R_MEAN), row(R_MEAN), ALU.mult)
                    v_r2 = vop("tensor_tensor", row(R_VAR), row(R_VAR), row(R_T1),
                               ALU.subtract, inc=True)
                    gd["lbank0"] = ("V", v_r2)
                    gd["lbank1"] = ("V", v_r2)
                    wait("scalar", "V", v_r2)
                    act(row(R_LN), row(R_VAR), AF.Ln, bias=epsb[0:1, 0:1], scale=1.0)
                    s_i2 = act(row(R_INV), row(R_LN), AF.Exp, scale=-0.5, inc=True)
                    wait("vector", "S", s_i2)
                    v_m2r = vop("tensor_tensor", row(R_MEAN), row(R_MEAN), row(R_INV),
                                ALU.mult, inc=True)
                    i2s, m2s = new_slot(), new_slot()
                    inv2_slot[qc] = i2s
                    minv2_slot[qc] = m2s
                    wait("sync", "V", v_m2r)
                    dma("sync", rbc[i2s:i2s + 1, :], row(R_INV), "RW")
                    ld2 = dma("sync", rbc[m2s:m2s + 1, :], row(R_MEAN), "RW")
                    wait("sync", "RW", ld2)
                    gd["rows1"] = [("S", s_i2), ("RW", ld2)]
                    wores_prev = wores
                    gd["xnn"] = ("V", cnt["V"])
                    gd["xnn_sq"] = ("T", cnt["T"])

                barrier()
                flush(blk)

        with ExitStack() as SB:
            fuse = SB.enter_context(nc.sbuf_tensor("fuse", [128, NICG * 1024], BF16))
            xn2 = SB.enter_context(nc.sbuf_tensor("xn2", [128, DC * 1024], BF16))
            gbuf = [SB.enter_context(nc.sbuf_tensor(f"gbuf{i}", [128, DC * 256], U8))
                    for i in range(2)]
            ubuf = [SB.enter_context(nc.sbuf_tensor(f"ubuf{i}", [128, DC * 256], U8))
                    for i in range(2)]
            dbuf = [SB.enter_context(nc.sbuf_tensor(f"dbuf{i}", [128, NICG * 128], U8))
                    for i in range(2)]
            inv2B = [SB.enter_context(nc.sbuf_tensor(f"inv2B{i}", [128, 512], F32))
                     for i in range(2)]
            minv2B = [SB.enter_context(nc.sbuf_tensor(f"minv2B{i}", [128, 512], F32))
                      for i in range(2)]
            x1f = [SB.enter_context(nc.sbuf_tensor(f"x1f{i}", [128, 512], F32))
                   for i in range(4)]
            x1g = [SB.enter_context(nc.sbuf_tensor(f"x1g{i}", [128, 512], F32))
                   for i in range(2)]
            fft = SB.enter_context(nc.sbuf_tensor("fft", [128, 512], F32))
            gact = [SB.enter_context(nc.sbuf_tensor(f"gact{i}", [128, 512], BF16))
                    for i in range(2)]
            outt = [SB.enter_context(nc.sbuf_tensor(f"outt{i}", [128, 512], F32))
                    for i in range(2)]

            def xn2_prep(hf):
                va = {0: {}, 1: {}}
                i2d = {}
                cls = []
                for tc2 in range(2):
                    qc = 2 * hf + tc2

                    def mk_i2(tc2=tc2, qc=qc):
                        def fn():
                            gwait("sync", gd.get(f"inv2B{tc2}"))
                            dma("sync", inv2B[tc2][:, :],
                                rbc[inv2_slot[qc]:inv2_slot[qc] + 1, :]
                                .broadcast_to([128, 512]), f"I2{tc2}")
                            i2d[tc2] = dma(
                                "sync", minv2B[tc2][:, :],
                                rbc[minv2_slot[qc]:minv2_slot[qc] + 1, :]
                                .broadcast_to([128, 512]), f"I2{tc2}")
                        return fn
                    cls.append(mk_i2())
                    for fc in range(FC):
                        def mk_tile(tc2=tc2, qc=qc, fc=fc):
                            def fn():
                                sl4 = (tc2 * FC + fc) % 4
                                gwait("sync", gd.get(f"x1f{sl4}"))
                                xfd = dma("sync", x1f[sl4][:, :],
                                          x1s[fc, :, qc * 512:(qc + 1) * 512], f"XF{sl4}")
                                wait("vector", f"XF{sl4}", xfd)
                                wait("vector", f"I2{tc2}", i2d[tc2])
                                gwait("vector", gd.get("xn2w"))
                                vop("tensor_tensor", fft[:, :], x1f[sl4][:, :],
                                    inv2B[tc2][:, :], ALU.mult)
                                v_a = vop("tensor_tensor",
                                          xn2[:, fc * 1024 + tc2 * 512:
                                              fc * 1024 + (tc2 + 1) * 512],
                                          fft[:, :], minv2B[tc2][:, :], ALU.subtract,
                                          inc=True)
                                gd[f"x1f{sl4}"] = ("V", v_a)
                                va[tc2][fc] = v_a
                                if fc == FC - 1:
                                    gd[f"inv2B{tc2}"] = ("V", v_a)
                            return fn
                        cls.append(mk_tile())
                return cls, va

            cls0, va0 = xn2_prep(0)
            for fn in cls0:
                fn()
            cls1, va1 = xn2_prep(1)
            dwd_h = {}
            for hf in range(2):
                va = va0 if hf == 0 else va1
                if hf == 1:
                    while cls1:
                        cls1.pop(0)()
                GB = [PS_S0, PS_S1]
                UB = [PS_C, PS_D]
                alt = 0
                for ich in range(ICH):
                    wbuf = ich % 2
                    gwait("gpsimd", gd.get(f"gbuf{wbuf}"))
                    dma("gpsimd", gbuf[wbuf].bitcast(FP8)[:, :],
                        gw[:, ich * 256:(ich + 1) * 256].rearrange("(dc p) f -> p dc f", p=128),
                        f"GU{wbuf}")
                    gud = dma("gpsimd", ubuf[wbuf].bitcast(FP8)[:, :],
                              uw[:, ich * 256:(ich + 1) * 256].rearrange("(dc p) f -> p dc f", p=128),
                              f"GU{wbuf}")
                    if ich in (ICH - 2, ICH - 1):
                        pre = ich - (ICH - 2)
                        gwait("gpsimd", gd.get(f"dbuf{pre}"))
                        dwd_h[pre] = dma(
                            "gpsimd", dbuf[pre].bitcast(FP8)[:, :],
                            dw[:, pre * 128:(pre + 1) * 128]
                            .rearrange("(icg p) f -> p icg f", p=128), f"DW{pre}")
                    for icsub in range(2):
                        icg = ich * 2 + icsub
                        for tc2 in range(2):
                            gb = GB[alt % 2]
                            ub = UB[alt % 2]
                            wait("tensor", f"GU{wbuf}", gud)
                            gwait("tensor", gd.get(f"gub{alt % 2}"))
                            if ich == 0 and icsub == 0:
                                wait("tensor", "V", va[tc2][FC - 1])
                            for dc in range(DC):
                                mm(ps[gb][:, :],
                                   gbuf[wbuf].bitcast(FP8)[:, dc * 256 + icsub * 128: dc * 256 + (icsub + 1) * 128],
                                   xn2[:, dc * 1024 + tc2 * 512: dc * 1024 + (tc2 + 1) * 512],
                                   dc == 0, dc == DC - 1)
                            t_g = tinc()
                            for dc in range(DC):
                                mm(ps[ub][:, :],
                                   ubuf[wbuf].bitcast(FP8)[:, dc * 256 + icsub * 128: dc * 256 + (icsub + 1) * 128],
                                   xn2[:, dc * 1024 + tc2 * 512: dc * 1024 + (tc2 + 1) * 512],
                                   dc == 0, dc == DC - 1)
                            t_u = tinc()
                            wait("scalar", "T", t_g)
                            gwait("scalar", gd.get(f"gact{alt % 2}"))
                            s_g = act(gact[alt % 2][:, :], ps[gb][:, :], AF.Silu,
                                      scale=gs_sb[:, icg:icg + 1], inc=True)
                            wait("vector", "T", t_u)
                            wait("vector", "S", s_g)
                            v_f = vop("scalar_tensor_tensor",
                                      fuse[:, icg * 1024 + tc2 * 512: icg * 1024 + (tc2 + 1) * 512],
                                      ps[ub][:, :], us_sb[:, icg:icg + 1], gact[alt % 2][:, :],
                                      ALU.mult, ALU.mult, inc=True)
                            gd[f"gub{alt % 2}"] = ("V", v_f)
                            gd[f"gact{alt % 2}"] = ("V", v_f)
                            alt += 1
                    gd[f"gbuf{wbuf}"] = ("T", cnt["T"])
                    v_fuse = cnt["V"]
                gd["xn2w"] = ("T", cnt["T"])

                DB = [PS_L0, PS_L1]
                for fc in range(FC):
                    wbuf = fc % 2
                    dwd = dwd_h[fc]
                    for tc2 in range(2):
                        qc = 2 * hf + tc2
                        gwait("sync", gd.get(f"x1g{tc2}"))
                        xgd = dma("sync", x1g[tc2][:, :],
                                  x1s[fc, :, qc * 512:(qc + 1) * 512], f"XG{tc2}")
                        gd[f"_xg{tc2}"] = (f"XG{tc2}", xgd)
                    wait("tensor", f"DW{wbuf}", dwd)
                    wait("tensor", "V", v_fuse)
                    gwait("tensor", gd.get("dbank0"))
                    gwait("tensor", gd.get("dbank1"))
                    for icg in range(NICG):
                        for tc2 in range(2):
                            mm(ps[DB[tc2]][:, :],
                               dbuf[wbuf].bitcast(FP8)[:, icg * 128:(icg + 1) * 128],
                               fuse[:, icg * 1024 + tc2 * 512: icg * 1024 + (tc2 + 1) * 512],
                               icg == 0, icg == NICG - 1)
                    t_dn = tinc()
                    gd[f"dbuf{wbuf}"] = ("T", t_dn)
                    if fc + 2 < FC:
                        gwait("gpsimd", gd.get(f"dbuf{wbuf}"))
                        dwd_h[fc + 2] = dma(
                            "gpsimd", dbuf[wbuf].bitcast(FP8)[:, :],
                            dw[:, (fc + 2) * 128:(fc + 3) * 128]
                            .rearrange("(icg p) f -> p icg f", p=128), f"DW{wbuf}")
                    for tc2 in range(2):
                        wait("vector", "T", t_dn)
                        gwait("vector", gd.get(f"_xg{tc2}"))
                        gwait("vector", gd.get(f"outt{tc2}"))
                        v_o = vop("scalar_tensor_tensor", outt[tc2][:, :],
                                  ps[DB[tc2]][:, :], ds_sb[:, fc:fc + 1], x1g[tc2][:, :],
                                  ALU.mult, ALU.add, inc=True)
                        gd[f"x1g{tc2}"] = ("V", v_o)
                        gd[f"dbank{tc2}"] = ("V", v_o)
                        wait("sync", "V", v_o)
                        od = dma("sync", out[fc * 128:(fc + 1) * 128,
                                           hf * 1024 + tc2 * 512: hf * 1024 + (tc2 + 1) * 512],
                                 outt[tc2][:, :], f"OUT{tc2}")
                        gd[f"outt{tc2}"] = (f"OUT{tc2}", od)
                    if hf == 0:
                        for _ in range(3):
                            if cls1:
                                cls1.pop(0)()

            for e, ss in OWN.items():
                for sn in ss:
                    wait(e, sn, cnt[sn])
            flush(blk)

    return nc


def _host_masks(parity):
    m8 = np.ones((8, 128, 512), np.float32)
    pidx = np.arange(128)[:, None]
    qidx = np.arange(128)[None, :]
    tri = np.where(qidx >= pidx, 1.0, 0.0).astype(np.float32)
    for m in range(8):
        o = m - parity
        if o < 0:
            continue
        for j in range(4):
            d = 2 * j - o
            if d < 0:
                m8[m][:, j * 128:(j + 1) * 128] = 0.0
            elif d == 0:
                m8[m][:, j * 128:(j + 1) * 128] = tri
    return m8


def kernel(L0_attn_x, L0_attn_Wq, L0_attn_Wk, L0_attn_Wv, L0_attn_Wo,
           L0_attn_q_gamma, L0_attn_k_gamma, L0_ffn_gate_w, L0_ffn_up_w,
           L0_ffn_down_w, L0_ffn_gate_scale, L0_ffn_up_scale, L0_ffn_down_scale):
    x = np.ascontiguousarray(np.asarray(L0_attn_x), dtype=np.float32)
    wq_ = np.ascontiguousarray(np.asarray(L0_attn_Wq), dtype=np.float32)
    wk_ = np.ascontiguousarray(np.asarray(L0_attn_Wk), dtype=np.float32)
    wv_ = np.ascontiguousarray(np.asarray(L0_attn_Wv), dtype=np.float32)
    wo_ = np.ascontiguousarray(np.asarray(L0_attn_Wo), dtype=np.float32)
    qg_ = np.ascontiguousarray(np.asarray(L0_attn_q_gamma), dtype=np.float32).reshape(128, 1)
    kg_ = np.ascontiguousarray(np.asarray(L0_attn_k_gamma), dtype=np.float32).reshape(128, 1)
    gw_ = np.ascontiguousarray(np.asarray(L0_ffn_gate_w), dtype=np.float32)
    uw_ = np.ascontiguousarray(np.asarray(L0_ffn_up_w), dtype=np.float32)
    dw_ = np.ascontiguousarray(np.asarray(L0_ffn_down_w), dtype=np.float32)
    gs_ = np.ascontiguousarray(np.asarray(L0_ffn_gate_scale), dtype=np.float32).reshape(NICG, 128).T.copy()
    us_ = np.ascontiguousarray(np.asarray(L0_ffn_up_scale), dtype=np.float32).reshape(NICG, 128).T.copy()
    ds_ = np.ascontiguousarray(np.asarray(L0_ffn_down_scale), dtype=np.float32).reshape(FC, 128).T.copy()

    xT_b = [np.ascontiguousarray(x[b].T) for b in range(4)]
    masks = [_host_masks(p) for p in range(2)]
    own_cols = [np.concatenate([np.arange(g * 128, (g + 1) * 128)
                                for g in range(p, 32, 2)]) for p in range(2)]

    nc = _build()
    in_maps = []
    for c in range(8):
        b, p = c // 2, c % 2
        in_maps.append({
            "xT": xT_b[b],
            "xq": np.ascontiguousarray(xT_b[b][:, own_cols[p]]),
            "wq": wq_, "wk": wk_, "wv": wv_, "wo": wo_,
            "gw": gw_, "uw": uw_, "dw": dw_,
            "qg": qg_, "kg": kg_,
            "gs": gs_, "us": us_, "dsc": ds_,
            "mask8": masks[p],
        })
    res = run_bass_kernel_spmd(nc, in_maps, core_ids=list(range(8)))
    outp = np.empty((4, S, D), np.float32)
    for c in range(8):
        b, p = c // 2, c % 2
        o = res.results[c]["out"]
        oT = np.ascontiguousarray(o.T)
        for j, g in enumerate(range(p, 32, 2)):
            outp[b, g * 128:(g + 1) * 128, :] = oT[j * 128:(j + 1) * 128, :]
    return outp


# version 25
# speedup vs baseline: 1.0178x; 1.0178x over previous
import numpy as np
from contextlib import ExitStack

import concourse.bass as bass
import concourse.mybir as mybir
from concourse.bass_utils import run_bass_kernel_spmd

F32 = mybir.dt.float32
F32R = mybir.dt.float32r
BF16 = mybir.dt.bfloat16
FP8 = mybir.dt.float8e4
U8 = mybir.dt.uint8
AF = mybir.ActivationFunctionType
ALU = mybir.AluOpType
AX = mybir.AxisListType

D = 2048
DC = 16
S = 4096
NT = 8
NH = 16
KVH = 4
DH = 128
TOK = 2048
NQC = 4
II = 6144
NICG = 48
ICH = 24
FC = 16
SCALE = DH ** -0.5
C_OFF = 44.0
NEG = -1.0e9

ENGS = ("sync", "gpsimd", "tensor", "vector", "scalar")


def _build():
    nc = bass.Bass("TRN2", target_bir_lowering=False)

    xT = nc.dram_tensor("xT", [D, S], F32, kind="ExternalInput").ap()
    xq = nc.dram_tensor("xq", [D, TOK], F32, kind="ExternalInput").ap()
    wq = nc.dram_tensor("wq", [D, D], F32, kind="ExternalInput").ap()
    wk = nc.dram_tensor("wk", [D, 512], F32, kind="ExternalInput").ap()
    wv = nc.dram_tensor("wv", [D, 512], F32, kind="ExternalInput").ap()
    wo = nc.dram_tensor("wo", [D, D], F32, kind="ExternalInput").ap()
    gw = nc.dram_tensor("gw", [D, II], F32, kind="ExternalInput").ap()
    uw = nc.dram_tensor("uw", [D, II], F32, kind="ExternalInput").ap()
    dw = nc.dram_tensor("dw", [II, D], F32, kind="ExternalInput").ap()
    qg = nc.dram_tensor("qg", [128, 1], F32, kind="ExternalInput").ap()
    kg = nc.dram_tensor("kg", [128, 1], F32, kind="ExternalInput").ap()
    gs = nc.dram_tensor("gs", [128, NICG], F32, kind="ExternalInput").ap()
    us = nc.dram_tensor("us", [128, NICG], F32, kind="ExternalInput").ap()
    dsc = nc.dram_tensor("dsc", [128, FC], F32, kind="ExternalInput").ap()
    mask8 = nc.dram_tensor("mask8", [8, 128, 512], F32, kind="ExternalInput").ap()
    out = nc.dram_tensor("out", [D, TOK], F32, kind="ExternalOutput").ap()
    x1s = nc.dram_tensor("x1s", [FC, 128, TOK], F32, kind="Internal").ap()
    rbc = nc.dram_tensor("rbc", [512, 512], F32, kind="Internal").ap()

    from collections import defaultdict
    cnt = defaultdict(int)
    sem = {}
    prog = {e: [] for e in ENGS}
    slot_ctr = [0]

    def new_slot():
        s = slot_ctr[0]
        slot_ctr[0] += 1
        assert s < 512
        return s

    def emit(e, fn):
        prog[e].append(fn)

    def wait(e, c, v):
        assert v <= cnt[c], f"wait on future: {e} {c} {v} > {cnt[c]}"
        if v <= 0:
            return
        emit(e, lambda eng, s=c, v=v: eng.wait_ge(sem[s], v))

    def gwait(e, g):
        if not g:
            return
        if isinstance(g, list):
            for gg in g:
                wait(e, gg[0], gg[1])
        else:
            wait(e, g[0], g[1])

    def dma(e, o, i, c):
        if e == "scalar" and cnt["S"] > 0:
            emit("scalar", lambda eng, v=cnt["S"]: eng.wait_ge(sem["S"], v))
        cnt[c] += 16
        emit(e, lambda eng, o=o, i=i, s=c: eng.dma_start(out=o, in_=i).then_inc(sem[s], 16))
        return cnt[c]

    def dma_nc(e, o, i, c):
        cnt[c] += 16

        def f(eng, o=o, i=i, sn=c):
            with nc.allow_non_contiguous_dma(reason="tiny one-time transpose"):
                return eng.dma_start(out=o, in_=i).then_inc(sem[sn], 16)
        emit(e, f)
        return cnt[c]

    def mm(o, l, r, st, sp, inc=False):
        if inc:
            cnt["T"] += 1
            emit("tensor", lambda eng, o=o, l=l, r=r, st=st, sp=sp:
                 eng.matmul(o, l, r, start=st, stop=sp).then_inc(sem["T"], 1))
            return cnt["T"]
        emit("tensor", lambda eng, o=o, l=l, r=r, st=st, sp=sp:
             eng.matmul(o, l, r, start=st, stop=sp))

    def act(o, i, f, bias=None, scale=None, inc=False):
        kw = {}
        if bias is not None:
            kw["bias"] = bias
        if scale is not None:
            kw["scale"] = scale
        if cnt["S"] > 0:
            emit("scalar", lambda eng, v=cnt["S"]: eng.wait_ge(sem["S"], v))
        cnt["S"] += 1
        emit("scalar", lambda eng, o=o, i=i, f=f, kw=kw:
             eng.activation(out=o, in_=i, func=f, **kw).then_inc(sem["S"], 1))
        return cnt["S"]

    def vop(name, *args, inc=False, **kw):
        if cnt["V"] > 0:
            emit("vector", lambda eng, v=cnt["V"]: eng.wait_ge(sem["V"], v))
        cnt["V"] += 1
        emit("vector", lambda eng, n=name, a=args, kw=kw:
             getattr(eng, n)(*a, **kw).then_inc(sem["V"], 1))
        return cnt["V"]

    OWN = {
        "sync": ["PRE", "X", 'X0', 'X1', 'X2', 'X3', 'X4', 'X5', 'X6', 'X7', 'X8', 'X9', 'X10', 'X11', 'X12', 'X13', 'X14', 'X15',
                 "SP0", "SP1", "XF0", "XF1", "XG0", "XG1", "OUT0", "OUT1",
                 "RW", "BI", "BR0", "BR1", "BB0", "BB1", "RL0", "RL1", "I20", "I21",
                 "XF2", "XF3"],
        "gpsimd": ["M", "WQ0", "WQ1", "WO0", "WO1", "GU0", "GU1", "DW0", "DW1"],
        "scalar": [],
    }

    def tinc():
        cnt["T"] += 1
        last = prog["tensor"].pop()
        emit("tensor", lambda eng, f=last: f(eng).then_inc(sem["T"], 1))
        return cnt["T"]

    ALLD = [sn for ss in (("PRE", "X", "SP0", "SP1", "XF0", "XF1", "XG0", "XG1",
                           "OUT0", "OUT1", "M", "WQ0", "WQ1", "WO0", "WO1", "GU0",
                           "GU1", "DW0", "DW1", "RW", "BI", "BR0", "BR1", "BB0",
                           "BB1", "RL0", "RL1", "I20", "I21"),) for sn in ss]

    def barrier():
        for e in ENGS:
            for sn in ("T", "V", "S"):
                wait(e, sn, cnt[sn])
            for sn in ALLD:
                wait(e, sn, cnt[sn])

    def flush(blk):
        for e in ENGS:
            fns = prog[e]
            if not fns:
                continue
            prog[e] = []
            getattr(blk, e)(lambda eng, fns=fns: [f(eng) for f in fns])

    with ExitStack() as TOP:
        for s in "PRE,M,X,X0,X1,X2,X3,X4,X5,X6,X7,X8,X9,X10,X11,X12,X13,X14,X15,RW,BI,BR0,BR1,BB0,BB1,RL0,RL1,SP0,SP1,WQ0,WQ1,WO0,WO1,I20,I21,XF0,XF1,XF2,XF3,GU0,GU1,DW0,DW1,XG0,XG1,OUT0,OUT1,T,V,S,P".split(","):
            sem[s] = TOP.enter_context(nc.semaphore(s))

        qg_sb = TOP.enter_context(nc.sbuf_tensor("qg_sb", [128, 1], F32))
        kg_sb = TOP.enter_context(nc.sbuf_tensor("kg_sb", [128, 1], F32))
        gs_sb = TOP.enter_context(nc.sbuf_tensor("gs_sb", [128, NICG], F32))
        us_sb = TOP.enter_context(nc.sbuf_tensor("us_sb", [128, NICG], F32))
        ds_sb = TOP.enter_context(nc.sbuf_tensor("ds_sb", [128, FC], F32))
        ones1 = TOP.enter_context(nc.sbuf_tensor("ones1", [128, 1], BF16))
        ones_r = TOP.enter_context(nc.sbuf_tensor("ones_r", [1, 128], F32R))
        rlrow = TOP.enter_context(nc.sbuf_tensor("rlrow", [1, 512], F32R))
        rrow2 = TOP.enter_context(nc.sbuf_tensor("rrow2", [1, 512], F32R))
        statk = TOP.enter_context(nc.sbuf_tensor("statk", [128, 2], BF16))
        statq = TOP.enter_context(nc.sbuf_tensor("statq", [128, 2], BF16))
        rowb = TOP.enter_context(nc.sbuf_tensor("rowb", [1, 7 * 512], F32))
        epsb = TOP.enter_context(nc.sbuf_tensor("epsb", [1, 4], F32))
        mkrow = TOP.enter_context(nc.sbuf_tensor("mkrow", [1, 16], F32))

        def row(i):
            return rowb[0:1, i * 512:(i + 1) * 512]

        ps = [TOP.enter_context(nc.psum_tensor(f"ps{i}", [128, 512], F32))
              for i in range(4)]
        lacc = TOP.enter_context(nc.psum_tensor("lacc", [128, 1024], F32))
        ps.append(lacc[:, 0:512])
        ps.append(lacc[:, 512:1024])
        ps.extend(TOP.enter_context(nc.psum_tensor(f"ps{i}", [128, 512], F32))
                  for i in range(6, 8))
        PS_S0, PS_S1, PS_C, PS_D, PS_L0, PS_L1, PS_G, PS_H = range(8)

        blk = TOP.enter_context(nc.Block())

        dma("sync", qg_sb[:, :], qg, "PRE")
        dma("sync", kg_sb[:, :], kg, "PRE")
        dma("sync", gs_sb[:, :], gs, "PRE")
        dma("sync", us_sb[:, :], us, "PRE")
        dma("sync", ds_sb[:, :], dsc, "PRE")
        pre_d = cnt["PRE"]
        wait("vector", "PRE", pre_d)
        vop("memset", ones1[:, :], 1.0)
        vop("memset", statk[:, 0:1], 1.0)
        vop("memset", statq[:, 0:1], 1.0)
        vop("tensor_tensor", statk[:, 1:2], kg_sb[:, :], kg_sb[:, :], ALU.mult)
        vop("tensor_tensor", statq[:, 1:2], qg_sb[:, :], qg_sb[:, :], ALU.mult)
        vop("memset", rowb[0:1, 6 * 512:6 * 512 + 128], 1.0)
        vop("tensor_copy", ones_r[:, :], rowb[0:1, 6 * 512:6 * 512 + 128])
        vop("memset", epsb[0:1, 0:1], 1e-5)
        vop("memset", epsb[0:1, 1:2], 1e-8)
        vop("memset", epsb[0:1, 2:3], 1e-35)
        vop("memset", mkrow[:, :], 0.0, inc=True)
        pre_v = cnt["V"]
        wait("scalar", "PRE", pre_d)

        with ExitStack() as SA:
            khat = SA.enter_context(nc.sbuf_tensor("khat", [128, KVH * S], BF16))
            vtok = SA.enter_context(nc.sbuf_tensor("vtok", [128, 32 * 512], BF16))
            mask_sb = SA.enter_context(nc.sbuf_tensor("mask_sb", [128, 8 * 512], BF16))
            xtt = SA.enter_context(nc.sbuf_tensor("xtt", [128, DC * 512], F32))
            xnn = SA.enter_context(nc.sbuf_tensor("xnn", [128, DC * 512], BF16))
            invB = SA.enter_context(nc.sbuf_tensor("invB", [128, 512], F32))
            minvB = SA.enter_context(nc.sbuf_tensor("minvB", [128, 512], F32))
            rall = SA.enter_context(nc.sbuf_tensor("rall", [128, KVH * 32], F32))
            mkacc = SA.enter_context(nc.sbuf_tensor("mkacc", [128, KVH], F32))
            eps128 = SA.enter_context(nc.sbuf_tensor("eps128", [128, 2], F32))
            sc4 = SA.enter_context(nc.sbuf_tensor("sc4", [128, 4], F32))
            r2t = SA.enter_context(nc.sbuf_tensor("r2t", [128, 4], F32))
            nbt = SA.enter_context(nc.sbuf_tensor("nbt", [128, 4], F32))
            mkt = SA.enter_context(nc.sbuf_tensor("mkt", [4, 128], F32))
            mk_dram = nc.dram_tensor("mk_dram", [128, 4], F32, kind="Internal").ap()
            mk_dram2 = nc.dram_tensor("mk_dram2", [1, 4], F32, kind="Internal").ap()

            dma("gpsimd", mask_sb[:, :], mask8.rearrange("m p t -> p m t"), "M")

            gd = {}

            with ExitStack() as SA1:
                wk_sb = SA1.enter_context(nc.sbuf_tensor("wk_sb", [128, DC * 512], BF16))
                wv_sb = SA1.enter_context(nc.sbuf_tensor("wv_sb", [128, DC * 512], BF16))
                ktmp = SA1.enter_context(nc.sbuf_tensor("ktmp", [128, 512], BF16))
                ksq = SA1.enter_context(nc.sbuf_tensor("ksq", [128, 512], BF16))
                xtmp = SA1.enter_context(nc.sbuf_tensor("xtmp", [128, 512], F32))
                rB = [SA1.enter_context(nc.sbuf_tensor(f"rB{i}", [128, 512], F32))
                      for i in range(2)]
                xnn2 = SA1.enter_context(nc.sbuf_tensor("xnn2", [128, DC * 512], BF16))
                xbuf = [xnn, xnn2]

                dma("gpsimd", wk_sb[:, :], wk.rearrange("(dc p) f -> p dc f", p=128), "M")
                dma("gpsimd", wv_sb[:, :], wv.rearrange("(dc p) f -> p dc f", p=128), "M")
                wkv_d = cnt["M"]

                R_MEAN, R_VAR, R_INV, R_LN, R_KR2, R_NBK, R_T1 = range(7)

                wait("tensor", "V", pre_v)
                vop("memset", eps128[:, 0:1], 1e-8)
                vop("memset", eps128[:, 1:2], -2.426015131959809)
                vop("memset", mkacc[:, :], 0.0, inc=True)
                def p1_rows(t_sumsq):
                    wait("vector", "T", t_sumsq)
                    gwait("vector", gd.get("rows1"))
                    vop("tensor_scalar", row(R_MEAN), ps[PS_L0][0:1, :], 1.0 / D, None, ALU.mult)
                    vop("tensor_scalar", row(R_VAR), ps[PS_L1][0:1, :], 1.0 / D, None, ALU.mult)
                    vop("tensor_tensor", row(R_T1), row(R_MEAN), row(R_MEAN), ALU.mult)
                    v_rows1 = vop("tensor_tensor", row(R_VAR), row(R_VAR), row(R_T1),
                                  ALU.subtract, inc=True)
                    gd["bank_l0"] = ("V", v_rows1)
                    gd["bank_l1"] = ("V", v_rows1)
                    wait("scalar", "V", v_rows1)
                    act(row(R_LN), row(R_VAR), AF.Ln, bias=epsb[0:1, 0:1], scale=1.0)
                    act(row(R_INV), row(R_LN), AF.Exp, scale=-0.5)
                    gwait("scalar", gd.get("rlrow_w"))
                    s_inv = act(rlrow[:, :], row(R_LN), AF.Exp, scale=-0.5, inc=True)
                    wait("vector", "S", s_inv)
                    gwait("vector", gd.get("r2w"))
                    v_minv = vop("tensor_tensor", rrow2[:, :], row(R_MEAN), row(R_INV),
                                 ALU.mult, inc=True)
                    wait("tensor", "S", s_inv)
                    wait("tensor", "V", v_minv)
                    mm(ps[PS_L0][:, :], ones_r[:, :], rlrow[:, :], True, True)
                    mm(ps[PS_L1][:, :], ones_r[:, :], rrow2[:, :], True, True)
                    t_lb = tinc()
                    gd["rlrow_w"] = ("T", t_lb)
                    gd["r2w"] = ("T", t_lb)
                    gd["rows1"] = ("T", t_lb)
                    wait("vector", "T", t_lb)
                    gwait("vector", gd.get("invB"))
                    vop("tensor_copy", invB[:, :], ps[PS_L0][:, :])
                    v_bi = vop("tensor_copy", minvB[:, :], ps[PS_L1][:, :], inc=True)
                    gd["bank_l0"] = ("V", v_bi)
                    gd["bank_l1"] = ("V", v_bi)

                xst = {}
                xls = []
                for dc in range(DC):
                    xls.append(dma("sync", xtt[:, dc * 512:(dc + 1) * 512],
                                   xT[:, 0:512].rearrange(
                                       "(dc p) t -> p dc t", p=128)[:, dc, :], f"X{dc}"))
                for t in range(NT):
                    xb = xbuf[t % 2]
                    if t == 0:
                        s_cp = []
                        for dc in range(DC):
                            wait("scalar", f"X{dc}", xls[dc])
                            s_cp.append(act(xb[:, dc * 512:(dc + 1) * 512],
                                            xtt[:, dc * 512:(dc + 1) * 512], AF.Copy))
                        tsx = []
                        for dc in range(DC):
                            wait("tensor", "S", s_cp[dc])
                            mm(ps[PS_L0][0:1, :], ones1[:, :], xb[:, dc * 512:(dc + 1) * 512],
                               dc == 0, dc == DC - 1)
                            tsx.append(tinc())
                        s_sq = []
                        for dc in range(DC):
                            wait("scalar", "T", tsx[dc])
                            s_sq.append(act(xb[:, dc * 512:(dc + 1) * 512],
                                            xb[:, dc * 512:(dc + 1) * 512], AF.Square))
                        for dc in range(DC):
                            wait("tensor", "S", s_sq[dc])
                            mm(ps[PS_L1][0:1, :], ones1[:, :], xb[:, dc * 512:(dc + 1) * 512],
                               dc == 0, dc == DC - 1)
                        p1_rows(tinc())


                    xn_ = xbuf[(t + 1) % 2]
                    v_cps = {}
                    v_sqs = {}
                    xls_next = []

                    def mk_copy(dc):
                        def fn():
                            wait("vector", f"X{dc}", xls_next[dc])
                            if dc == 0:
                                gwait("vector", gd.get("xnn_sq"))
                            v_cps[dc] = vop("tensor_copy",
                                            xn_[:, dc * 512:(dc + 1) * 512],
                                            xtt[:, dc * 512:(dc + 1) * 512], inc=True)
                        return fn

                    def sum_mms():
                        gwait("tensor", gd.get("bank_l0"))
                        gwait("tensor", gd.get("bank_l1"))
                        for dc in range(DC):
                            wait("tensor", "V", v_cps[dc])
                            mm(ps[PS_L0][0:1, :], ones1[:, :],
                               xn_[:, dc * 512:(dc + 1) * 512], dc == 0, dc == DC - 1)
                        xst["t_sum"] = tinc()

                    def mk_sq(dc):
                        def fn():
                            if dc == 0:
                                wait("vector", "T", xst["t_sum"])
                            v_sqs[dc] = vop("tensor_tensor",
                                            xn_[:, dc * 512:(dc + 1) * 512],
                                            xn_[:, dc * 512:(dc + 1) * 512],
                                            xn_[:, dc * 512:(dc + 1) * 512],
                                            ALU.mult, inc=True)
                        return fn

                    def sq_mms():
                        for dc in range(DC):
                            wait("tensor", "V", v_sqs[dc])
                            mm(ps[PS_L1][0:1, :], ones1[:, :],
                               xn_[:, dc * 512:(dc + 1) * 512], dc == 0, dc == DC - 1)
                        xst["t_sumsq"] = tinc()
                    if t < NT - 1:
                        pcls = [mk_copy(dc) for dc in range(DC)]
                        pcls.append(sum_mms)
                        pcls.extend(mk_sq(dc) for dc in range(DC))
                        pcls.append(sq_mms)
                        pcls.append(lambda: p1_rows(xst.pop("t_sumsq")))
                    else:
                        pcls = []

                    gwait("vector", gd.get("xnn"))
                    vaff = []
                    vsub = []
                    for dc in range(DC):
                        va = vop("tensor_tensor", xtmp[:, :], xtt[:, dc * 512:(dc + 1) * 512],
                                 invB[:, :], ALU.mult)
                        vaff.append(va)
                        vsub.append(vop("tensor_tensor", xb[:, dc * 512:(dc + 1) * 512],
                                        xtmp[:, :], minvB[:, :], ALU.subtract, inc=True))
                        if t < NT - 1:
                            wait("sync", "V", va)
                            xls_next.append(dma(
                                "sync", xtt[:, dc * 512:(dc + 1) * 512],
                                xT[:, (t + 1) * 512:(t + 2) * 512].rearrange(
                                    "(dc p) t -> p dc t", p=128)[:, dc, :], f"X{dc}"))
                        if dc >= 4 and pcls:
                            pcls.pop(0)()
                    v_xn = cnt["V"]
                    gd["invB"] = ("V", v_xn)

                    KB = [PS_S0, PS_S1, PS_G, PS_H]
                    if t == 0:
                        wait("tensor", "M", wkv_d)
                    t_kp = {}
                    for kvh in range(KVH):
                        b = KB[kvh]
                        gwait("tensor", gd.get(f"kbank{kvh}"))
                        for dc in range(DC):
                            if kvh == 0:
                                wait("tensor", "V", vsub[dc])
                            mm(ps[b][:, :], wk_sb[:, dc * 512 + kvh * 128: dc * 512 + (kvh + 1) * 128],
                               xb[:, dc * 512:(dc + 1) * 512], dc == 0, dc == DC - 1)
                        t_kp[kvh] = tinc()
                        if kvh >= 1:
                            for _ in range((4, 1, 4)[kvh - 1]):
                                if pcls:
                                    pcls.pop(0)()

                    for ts_ in range(4):
                        b = PS_C if ts_ % 2 == 0 else PS_D
                        gwait("tensor", gd.get(f"vbank{ts_ % 2}"))
                        if ts_ < 2:
                            gwait("tensor", gd.get("kstatbank"))
                        for dc in range(DC):
                            mm(ps[b][:, :],
                               xb[:, dc * 512 + ts_ * 128: dc * 512 + (ts_ + 1) * 128],
                               wv_sb[:, dc * 512:(dc + 1) * 512], dc == 0, dc == DC - 1)
                        t_vp = tinc()
                        for _ in range(3):
                            if pcls:
                                pcls.pop(0)()
                        wait("vector", "T", t_vp)
                        kt_idx = t * 4 + ts_
                        v_vc = vop("tensor_copy", vtok[:, kt_idx * 512:(kt_idx + 1) * 512],
                                   ps[b][:, :], inc=True)
                        gd[f"vbank{ts_ % 2}"] = ("V", v_vc)
                        v_vc_last = v_vc

                    for kvh in range(KVH):
                        b = KB[kvh]
                        wait("scalar", "T", t_kp[kvh])
                        gwait("scalar", gd.get("ksq"))
                        act(ktmp[:, :], ps[b][:, :], AF.Copy)
                        s_ksq = act(ksq[:, :], ktmp[:, :], AF.Square, inc=True)
                        wait("tensor", "S", s_ksq)
                        wait("tensor", "V", v_vc_last)
                        gwait("tensor", gd.get("kstatbank"))
                        for ts_ in range(4):
                            mm(ps[PS_C][:, kvh * 4 + ts_: kvh * 4 + ts_ + 1],
                               ksq[:, ts_ * 128:(ts_ + 1) * 128], ones1[:, :], True, True)
                            mm(ps[PS_D][:, kvh * 4 + ts_: kvh * 4 + ts_ + 1],
                               ksq[:, ts_ * 128:(ts_ + 1) * 128], statk[:, 1:2], True, True)
                        t_ks = tinc()
                        gd["ksq"] = ("T", t_ks)
                        wait("scalar", "T", t_ks)
                        gwait("scalar", gd.get("sc4"))
                        s_ln = act(sc4[:, :], ps[PS_C][:, kvh * 4: kvh * 4 + 4], AF.Ln,
                                   bias=eps128[:, 0:1], scale=1.0 / DH, inc=True)
                        act(rall[:, kvh * 32 + t * 4: kvh * 32 + t * 4 + 4], sc4[:, :],
                            AF.Exp, scale=-0.5, bias=eps128[:, 1:2])
                        s_kr = act(r2t[:, :], sc4[:, :], AF.Exp, scale=-1.0, inc=True)
                        gd["sc4"] = ("S", s_kr)
                        wait("vector", "S", s_kr)
                        vop("tensor_tensor", nbt[:, :], ps[PS_D][:, kvh * 4: kvh * 4 + 4],
                            r2t[:, :], ALU.mult)
                        vop("tensor_reduce", nbt[:, 0:1], nbt[:, :], AX.X, ALU.max)
                        v_nb = vop("tensor_tensor", mkacc[:, kvh:kvh + 1],
                                   mkacc[:, kvh:kvh + 1], nbt[:, 0:1], ALU.max, inc=True)
                        gd["kstatbank"] = ("V", v_nb)
                        v_kh = vop("tensor_scalar",
                                   khat[:, kvh * S + t * 512: kvh * S + (t + 1) * 512],
                                   ps[b][:, :], kg_sb[:, :], None, ALU.mult, inc=True)
                        gd[f"kbank{kvh}"] = ("V", v_kh)
                        for _ in range(2):
                            if pcls:
                                pcls.pop(0)()
                    while pcls:
                        pcls.pop(0)()
                    gd["xnn"] = ("T", cnt["T"])
                    gd["xnn_sq"] = ("T", cnt["T"])

                wait("sync", "V", cnt["V"])
                mkd = dma("sync", mk_dram, mkacc[:, :], "X")
                wait("sync", "X", mkd)
                mkd2 = dma_nc("sync", mkt[:, :], mk_dram.rearrange("p k -> k p"), "X")
                wait("vector", "X", mkd2)
                v_mk4 = vop("tensor_reduce", nbt[0:4, 0:1], mkt[:, :], AX.X,
                            ALU.max, inc=True)
                wait("sync", "V", v_mk4)
                mkd3 = dma_nc("sync", mk_dram2.rearrange("one k -> k one"), nbt[0:4, 0:1], "X")
                wait("sync", "X", mkd3)
                mkd4 = dma("sync", mkrow[0:1, 0:4], mk_dram2, "X")

                barrier()
                flush(blk)

            with ExitStack() as SA2:
                qhat = [SA2.enter_context(nc.sbuf_tensor(f"qhat{i}", [128, 1024], BF16))
                        for i in range(2)]
                attno = SA2.enter_context(nc.sbuf_tensor("attno", [128, NH * 512], BF16))
                wqhg = [SA2.enter_context(nc.sbuf_tensor(f"wqhg{i}", [128, DC * 256], BF16))
                        for i in range(2)]
                wofc = [SA2.enter_context(nc.sbuf_tensor(f"wofc{i}", [128, DC * 128], BF16))
                        for i in range(2)]
                pbt = SA2.enter_context(nc.sbuf_tensor("pbt", [128, 3072], BF16))

                def pslot(sl, hp):
                    return pbt[:, sl * 1024 + hp * 512: sl * 1024 + (hp + 1) * 512]
                qraw = [SA2.enter_context(nc.sbuf_tensor(f"qraw{i}", [128, 512], BF16))
                        for i in range(2)]
                qsq = SA2.enter_context(nc.sbuf_tensor("qsq", [128, 512], BF16))
                rBq = [SA2.enter_context(nc.sbuf_tensor(f"rBq{i}", [128, 512], BF16))
                       for i in range(2)]
                rlB = [SA2.enter_context(nc.sbuf_tensor(f"rlB{i}", [128, 512], F32))
                       for i in range(2)]
                biasB = SA2.enter_context(nc.sbuf_tensor("biasB", [128, 4], F32))
                x1t = [SA2.enter_context(nc.sbuf_tensor(f"x1t{i}", [128, 512], F32))
                       for i in range(2)]



                R_MEAN, R_VAR, R_INV, R_LN, R_QR2, R_NBQ, R_T1 = range(7)
                wait("vector", "M", 48)
                inv2_slot = {}
                minv2_slot = {}

                preload = [None]
                xst = {}
                for qc in range(NQC):
                    K_QC = 8 * qc + 8
                    if preload[0] is not None:
                        xls = preload[0]
                    else:
                        xls = []
                        for dc in range(DC):
                            xls.append(dma("sync", xtt[:, dc * 512:(dc + 1) * 512],
                                           xq[:, qc * 512:(qc + 1) * 512].rearrange(
                                               "(dc p) t -> p dc t", p=128)[:, dc, :], f"X{dc}"))
                    if "t_sumsq" in xst:
                        t_sumsq = xst.pop("t_sumsq")
                    else:
                        s_cp = []
                        for dc in range(DC):
                            wait("scalar", f"X{dc}", xls[dc])
                            if dc == 0:
                                gwait("scalar", gd.get("xnn_sq"))
                            s_cp.append(act(xnn[:, dc * 512:(dc + 1) * 512],
                                            xtt[:, dc * 512:(dc + 1) * 512], AF.Copy))
                        tsx = []
                        for dc in range(DC):
                            wait("tensor", "S", s_cp[dc])
                            if dc == 0:
                                gwait("tensor", gd.get("av10"))
                                gwait("tensor", gd.get("av11"))
                            mm(ps[PS_G][0:1, :], ones1[:, :], xnn[:, dc * 512:(dc + 1) * 512],
                               dc == 0, dc == DC - 1)
                            tsx.append(tinc())
                        s_sq = []
                        for dc in range(DC):
                            wait("scalar", "T", tsx[dc])
                            s_sq.append(act(xnn[:, dc * 512:(dc + 1) * 512],
                                            xnn[:, dc * 512:(dc + 1) * 512], AF.Square))
                        for dc in range(DC):
                            wait("tensor", "S", s_sq[dc])
                            mm(ps[PS_H][0:1, :], ones1[:, :], xnn[:, dc * 512:(dc + 1) * 512],
                               dc == 0, dc == DC - 1)
                        t_sumsq = tinc()
                    wait("vector", "T", t_sumsq)
                    gwait("vector", gd.get("rows1"))
                    vop("tensor_scalar", row(R_MEAN), ps[PS_G][0:1, :], 1.0 / D, None, ALU.mult)
                    vop("tensor_scalar", row(R_VAR), ps[PS_H][0:1, :], 1.0 / D, None, ALU.mult)
                    gwait("vector", gd.get("t1m"))
                    vop("tensor_tensor", row(R_T1), row(R_MEAN), row(R_MEAN), ALU.mult)
                    v_rows1 = vop("tensor_tensor", row(R_VAR), row(R_VAR), row(R_T1),
                                  ALU.subtract, inc=True)
                    gd["av10"] = ("V", v_rows1)
                    gd["av11"] = ("V", v_rows1)
                    wait("scalar", "V", v_rows1)
                    act(row(R_LN), row(R_VAR), AF.Ln, bias=epsb[0:1, 0:1], scale=1.0)
                    act(row(R_INV), row(R_LN), AF.Exp, scale=-0.5)
                    gwait("scalar", gd.get("rlrow_w"))
                    s_inv = act(rlrow[:, :], row(R_LN), AF.Exp, scale=-0.5, inc=True)
                    wait("vector", "S", s_inv)
                    gwait("vector", gd.get("r2w"))
                    v_minv = vop("tensor_tensor", rrow2[:, :], row(R_MEAN), row(R_INV),
                                 ALU.mult, inc=True)
                    wait("tensor", "S", s_inv)
                    wait("tensor", "V", v_minv)
                    gwait("tensor", gd.get("av00"))
                    gwait("tensor", gd.get("av01"))
                    mm(ps[PS_C][:, :], ones_r[:, :], rlrow[:, :], True, True)
                    mm(ps[PS_D][:, :], ones_r[:, :], rrow2[:, :], True, True)
                    t_lb = tinc()
                    gd["rlrow_w"] = ("T", t_lb)
                    gd["r2w"] = ("T", t_lb)
                    gd["rows1"] = ("T", t_lb)
                    wait("vector", "T", t_lb)
                    gwait("vector", gd.get("invB"))
                    vop("tensor_copy", invB[:, :], ps[PS_C][:, :])
                    v_bi = vop("tensor_copy", minvB[:, :], ps[PS_D][:, :], inc=True)
                    gd["av00"] = ("V", v_bi)
                    gd["av01"] = ("V", v_bi)
                    gwait("vector", gd.get("xnn"))
                    gwait("vector", gd.get("rlB0"))
                    vsubq = []
                    for dc in range(DC):
                        vop("tensor_tensor", rlB[0][:, :], xtt[:, dc * 512:(dc + 1) * 512],
                            invB[:, :], ALU.mult)
                        vsubq.append(vop("tensor_tensor", xnn[:, dc * 512:(dc + 1) * 512],
                                         rlB[0][:, :], minvB[:, :], ALU.subtract, inc=True))
                    v_xn = cnt["V"]
                    gd["invB"] = ("V", v_xn)
                    gd["rlB0"] = ("V", v_xn)

                    P = {}

                    def part1(hg):
                        wb_ = hg % 2
                        PAIR = [PS_C, PS_D] if hg % 2 == 0 else [PS_G, PS_H]
                        st = {"vqh": {}, "vqc": {}, "vqsq": {}, "hg": hg}
                        gwait("gpsimd", gd.get(f"wqhg{wb_}"))
                        wqd = dma("gpsimd", wqhg[wb_][:, :],
                                  wq[:, hg * 256:(hg + 1) * 256].rearrange("(dc p) f -> p dc f", p=128),
                                  f"WQ{wb_}")
                        for hp in range(2):
                            qb = PAIR[hp]
                            wait("tensor", f"WQ{wb_}", wqd)
                            if hg > 0 or hp > 0:
                                wait("tensor", "V", v_xn)
                            gwait("tensor", gd.get(f"av{hg % 2}{hp}"))
                            for dc in range(DC):
                                if hg == 0 and hp == 0:
                                    wait("tensor", "V", vsubq[dc])
                                mm(ps[qb][:, :],
                                   wqhg[wb_][:, dc * 256 + hp * 128: dc * 256 + (hp + 1) * 128],
                                   xnn[:, dc * 512:(dc + 1) * 512], dc == 0, dc == DC - 1)
                            t_qp = tinc()
                            wait("vector", "T", t_qp)
                            gwait("vector", gd.get(f"qraw{hp}"))
                            st["vqc"][hp] = vop("tensor_copy", qraw[hp][:, :], ps[qb][:, :],
                                                inc=True)
                        gd[f"wqhg{wb_}"] = ("T", cnt["T"])
                        P[hg] = st

                    def part2(hg, hp):
                        wb_ = hg % 2
                        PAIR = [PS_C, PS_D] if hg % 2 == 0 else [PS_G, PS_H]
                        st = P[hg]
                        H = hg * 2 + hp
                        kvh = H // 4
                        wait("vector", "V", st["vqc"][hp])
                        gwait("vector", gd.get("qsq"))
                        v_qsq = vop("tensor_tensor", qsq[:, :], qraw[hp][:, :],
                                    qraw[hp][:, :], ALU.mult, inc=True)
                        wait("tensor", "V", v_qsq)
                        if hp == 1:
                            wait("tensor", "S", st["sqr0"])
                            wait("tensor", "V", st["vm20"])
                            wait("tensor", "V", st["vrbq0"])
                        mm(ps[PAIR[0]][0:1, :], ones1[:, :], qsq[:, :], True, True)
                        mm(ps[PAIR[1]][0:1, :], statq[:, 1:2], qsq[:, :], True, True)
                        t_qs = tinc()
                        gd["qsq"] = ("T", t_qs)
                        wait("scalar", "T", t_qs)
                        gwait("scalar", gd.get("qrow"))
                        act(row(R_LN), ps[PAIR[0]][0:1, :], AF.Ln, bias=epsb[0:1, 1:2], scale=1.0 / DH)
                        gwait("scalar", gd.get("rlrow_w"))
                        act(rlrow[:, :], row(R_LN), AF.Exp, scale=-0.5)
                        s_qr = act(row(R_QR2), row(R_LN), AF.Exp, scale=-1.0, inc=True)
                        wait("vector", "S", s_qr)
                        vop("tensor_tensor", row(R_NBQ), ps[PAIR[1]][0:1, :], row(R_QR2), ALU.mult)
                        vop("tensor_reduce", row(R_T1)[0:1, 0:1], row(R_NBQ), AX.X, ALU.max)
                        gwait("vector", gd.get("t1m"))
                        v_m2 = vop("tensor_tensor", row(R_T1)[0:1, 1:2], row(R_T1)[0:1, 0:1],
                                   mkrow[0:1, kvh:kvh + 1], ALU.mult, inc=True)
                        wait("scalar", "V", v_m2)
                        act(row(R_T1)[0:1, 2:3], row(R_T1)[0:1, 1:2], AF.Ln,
                            bias=epsb[0:1, 2:3], scale=1.0 / DH)
                        s_mv = act(row(R_T1)[0:1, 3:4], row(R_T1)[0:1, 2:3], AF.Exp,
                                   scale=0.5, inc=True)
                        gd["t1m"] = ("S", s_mv)
                        wait("vector", "S", s_mv)
                        gwait("vector", gd.get("biasrow"))
                        v_bias = vop("tensor_scalar", mkrow[0:1, 8 + hp:9 + hp],
                                     row(R_T1)[0:1, 3:4], -1.0, C_OFF,
                                     ALU.mult, op1=ALU.add, inc=True)
                        if hp == 0:
                            st["sqr0"] = s_qr
                            st["vm20"] = v_m2
                        st[f"p2c{hp}"] = (s_qr, v_m2, v_bias)

                    def part2b(hg, hp):
                        wb_ = hg % 2
                        PAIR = [PS_C, PS_D] if hg % 2 == 0 else [PS_G, PS_H]
                        st = P[hg]
                        s_qr, v_m2, v_bias = st[f"p2c{hp}"]
                        wait("tensor", "S", s_qr)
                        wait("tensor", "V", v_m2)
                        mm(ps[PAIR[hp]][:, :], ones_r[:, :], rlrow[:, :], True, True)
                        t_qb = tinc()
                        gd["rlrow_w"] = ("T", t_qb)
                        wait("vector", "T", t_qb)
                        gwait("vector", gd.get(f"rBq{hp}"))
                        v_rbq = vop("tensor_copy", rBq[hp][:, :], ps[PAIR[hp]][:, :], inc=True)
                        gwait("vector", gd.get(f"qhat{wb_}") if hp == 0 else None)
                        v_qh = vop("scalar_tensor_tensor",
                                   qhat[wb_][:, hp * 512:(hp + 1) * 512],
                                   qraw[hp][:, :], qg_sb[:, :], rBq[hp][:, :],
                                   ALU.mult, ALU.mult, inc=True)
                        gd[f"rBq{hp}"] = ("V", v_qh)
                        gd[f"qraw{hp}"] = ("V", v_qh)
                        st["vqh"][hp] = v_qh
                        if hp == 0:
                            st["vrbq0"] = v_rbq
                        else:
                            st["statdone"] = ("V", v_rbq)
                            slb = new_slot()
                            wait("sync", "V", v_bias)
                            wbb = dma("sync", rbc[slb:slb + 1, 0:2], mkrow[0:1, 8:10], "RW")
                            wait("sync", "RW", wbb)
                            gd["biasrow"] = ("RW", wbb)
                            gwait("sync", gd.get(f"biasB{wb_}"))
                            st["bbd"] = dma("sync", biasB[:, wb_ * 2:wb_ * 2 + 2],
                                            rbc[slb:slb + 1, 0:2].broadcast_to([128, 2]),
                                            f"BB{wb_}")

                    def attention(hg, inject=(), pre=()):
                        wb_ = hg % 2
                        st = P[hg]
                        kvh = hg // 2
                        AVB = [PS_C, PS_D] if hg % 2 == 0 else [PS_G, PS_H]
                        LB = [PS_L0, PS_L1]
                        SBK = [PS_S0, PS_S1]
                        KB = 8 * qc
                        s_exp = {0: {}, 1: {}}
                        t_cmb = {0: {}, 1: {}}
                        v_mask = {0: {}, 1: {}}
                        v_pair = {0: {}, 1: {}}
                        inj = {}
                        for kk, fn in inject:
                            inj.setdefault(kk, []).append(fn)
                        prei = {}
                        for kk, fn in pre:
                            prei.setdefault(kk, []).append(fn)
                        for k in range(K_QC + 1):
                            for fn in prei.pop(k, ()):
                                fn()
                            for hp in range(2):
                                H = hg * 2 + hp
                                qv = qhat[wb_][:, hp * 512:(hp + 1) * 512]
                                if k >= 1:
                                    j = k - 1
                                    if v_mask[hp][j]:
                                        wait("tensor", "V", v_mask[hp][j])
                                    else:
                                        wait("tensor", "S", s_exp[hp][j])
                                    if k == 1:
                                        gwait("tensor", gd.get(f"av{hg % 2}{hp}"))
                                        gwait("tensor", st.get("statdone"))
                                        gwait("tensor", gd.get(f"lbank{hp}"))
                                    mm(ps[AVB[hp]][:, :],
                                       vtok[:, j * 512 + kvh * 128: j * 512 + (kvh + 1) * 128],
                                       pslot(j % 2, hp), j == 0, j == K_QC - 1)
                                    if k >= 3 and k % 2 == 1:
                                        wait("tensor", "V", v_pair[hp][k - 2])
                                        mm(ps[LB[hp]][0:1, :], ones1[:, :],
                                           pslot(2, hp), k == 3, False)
                                    if k == K_QC:
                                        wait("tensor", "V", v_pair[hp][k - 1])
                                        mm(ps[LB[hp]][0:1, :], ones1[:, :],
                                           pslot(2, hp), False, True)
                                if k < K_QC:
                                    if k == 0:
                                        gwait("tensor", gd.get(f"sbank{hp}"))
                                        wait("tensor", "V", st["vqh"][hp])
                                    mm(ps[SBK[hp]][:, :],
                                       khat[:, kvh * S + k * 128: kvh * S + (k + 1) * 128],
                                       qv, True, True)
                                t_cmb[hp][k] = tinc()
                                if k < K_QC:
                                    wait("scalar", "T", t_cmb[hp][k])
                                    if k == 0:
                                        wait("scalar", f"BB{wb_}", st["bbd"])
                                        gwait("scalar", gd.get(f"pb_all{hp}"))
                                    if k >= 2:
                                        wait("scalar", "V",
                                             v_pair[hp][k - 1 if k % 2 == 0 else k - 2])
                                    s_exp[hp][k] = act(
                                        pslot(k % 2, hp), ps[SBK[hp]][:, :], AF.Exp,
                                        bias=biasB[:, wb_ * 2 + hp: wb_ * 2 + hp + 1],
                                        scale=rall[:, kvh * 32 + k: kvh * 32 + k + 1],
                                        inc=True)
                                    if k >= KB:
                                        mi = k - KB
                                        wait("vector", "S", s_exp[hp][k])
                                        v_mask[hp][k] = vop(
                                            "tensor_tensor", pslot(k % 2, hp),
                                            pslot(k % 2, hp),
                                            mask_sb[:, mi * 512:(mi + 1) * 512],
                                            ALU.mult, inc=True)
                                    else:
                                        v_mask[hp][k] = None
                                    if k % 2 == 1:
                                        if not v_mask[hp][k]:
                                            wait("vector", "S", s_exp[hp][k])
                                        v_pair[hp][k] = vop(
                                            "tensor_tensor", pslot(2, hp),
                                            pslot(k % 2, hp),
                                            pslot((k - 1) % 2, hp), ALU.add,
                                            inc=True)
                            for fn in inj.get(k, ()):
                                fn()
                        for fns in prei.values():
                            for fn in fns:
                                fn()
                        for hp in range(2):
                            gd[f"sbank{hp}"] = ("S", s_exp[hp][K_QC - 1])
                            gd[f"pb_all{hp}"] = ("T", t_cmb[hp][K_QC])
                        wait("scalar", "T", t_cmb[0][K_QC])
                        gwait("scalar", gd.get("rlrow_w"))
                        act(row(R_LN), ps[LB[0]][0:1, :], AF.Ln, scale=1.0)
                        s_rl0 = act(rlrow[:, :], row(R_LN), AF.Exp, scale=-1.0, inc=True)

                        def mk_fin(hp, s_rl_in):
                            pr = hg % 2

                            def fn():
                                if hp == 1:
                                    wait("scalar", "T", t_cmb[1][K_QC])
                                    gwait("scalar", gd.get("rlrow_w"))
                                    act(row(R_LN), ps[LB[1]][0:1, :], AF.Ln, scale=1.0)
                                    s_rl = act(rlrow[:, :], row(R_LN), AF.Exp, scale=-1.0,
                                               inc=True)
                                else:
                                    s_rl = s_rl_in
                                wait("tensor", "S", s_rl)
                                mm(ps[LB[hp]][:, :], ones_r[:, :], rlrow[:, :], True, True)
                                t_b = tinc()
                                gd["rlrow_w"] = ("T", t_b)
                                wait("vector", "T", t_b)
                                gwait("vector", gd.get(f"rlB{hp}"))
                                v_cp = vop("tensor_copy", rlB[hp][:, :], ps[LB[hp]][:, :],
                                           inc=True)
                                gd[f"lbank{hp}"] = ("V", v_cp)
                                wait("vector", "T", t_cmb[hp][K_QC])
                                v_ev = vop("tensor_tensor",
                                           attno[:, (hg * 2 + hp) * 512:(hg * 2 + hp + 1) * 512],
                                           ps[AVB[hp]][:, :], rlB[hp][:, :], ALU.mult,
                                           inc=True)
                                gd[f"av{pr}{hp}"] = ("V", v_ev)
                                gd[f"rlB{hp}"] = ("V", v_ev)
                            return fn

                        defer = ((0, mk_fin(0, s_rl0)), (1, mk_fin(1, None)))
                        gd[f"biasB{wb_}"] = ("S", cnt["S"])
                        gd[f"qhat{wb_}"] = ("T", cnt["T"])
                        return defer

                    if K_QC <= 8:
                        ka, kb, kc, kd = 4, 6, 6, 8
                    else:
                        ka, kb, kc, kd = 5, 7, 8, 10
                    part1(0)
                    part2(0, 0)
                    part2b(0, 0)
                    part2(0, 1)
                    part2b(0, 1)
                    evd = ()
                    for hg in range(8):
                        if hg < 7:
                            evd = attention(hg,
                                            inject=((2, lambda h=hg + 1: part1(h)),
                                                    (ka, lambda h=hg + 1: part2(h, 0)),
                                                    (kb, lambda h=hg + 1: part2b(h, 0)),
                                                    (kc, lambda h=hg + 1: part2(h, 1)),
                                                    (kd, lambda h=hg + 1: part2b(h, 1))),
                                            pre=evd)
                        else:
                            evd = attention(hg, pre=evd)
                    for _k, fn in evd:
                        fn()
                    v_evall = cnt["V"]

                    v_x1b_of = {}
                    def ln2_stats(i):
                        wait("tensor", "V", v_x1b_of[i])
                        if i == 0:
                            gwait("tensor", gd.get("lbank0"))
                            gwait("tensor", gd.get("lbank1"))
                        mm(ps[PS_L0][0:1, :], ones1[:, :], qraw[0][:, :], i == 0, i == FC - 1)
                        mm(ps[PS_L1][0:1, :], ones1[:, :], qsq[:, :], i == 0, i == FC - 1)
                        t_st = tinc()
                        gd["qsq"] = ("T", t_st)
                        gd["qraw0"] = ("T", t_st)
                        return t_st

                    def mk_xstat(dc):
                        def fn():
                            wait("vector", f"X{dc}", preload[0][dc])
                            v_cp = vop("tensor_copy", xnn[:, dc * 512:(dc + 1) * 512],
                                       xtt[:, dc * 512:(dc + 1) * 512], inc=True)
                            wait("tensor", "V", v_cp)
                            if dc == 0:
                                gwait("tensor", gd.get("av10"))
                                gwait("tensor", gd.get("av11"))
                            mm(ps[PS_G][0:1, :], ones1[:, :],
                               xnn[:, dc * 512:(dc + 1) * 512], dc == 0, dc == DC - 1)
                            t_s = tinc()
                            wait("vector", "T", t_s)
                            v_sq = vop("tensor_tensor", xnn[:, dc * 512:(dc + 1) * 512],
                                       xnn[:, dc * 512:(dc + 1) * 512],
                                       xnn[:, dc * 512:(dc + 1) * 512], ALU.mult, inc=True)
                            wait("tensor", "V", v_sq)
                            mm(ps[PS_H][0:1, :], ones1[:, :],
                               xnn[:, dc * 512:(dc + 1) * 512], dc == 0, dc == DC - 1)
                            xst["t_sumsq"] = tinc()
                        return fn
                    xstat_cls = ([mk_xstat(dc) for dc in range(DC)]
                                 if qc < NQC - 1 else [])
                    for fc in range(FC):
                        wb_ = fc % 2
                        gwait("gpsimd", gd.get(f"wofc{wb_}"))
                        wod = dma("gpsimd", wofc[wb_][:, :],
                                  wo[:, fc * 128:(fc + 1) * 128].rearrange("(dc p) f -> p dc f", p=128),
                                  f"WO{wb_}")
                        b = PS_S0 if fc % 2 == 0 else PS_S1
                        wait("tensor", f"WO{wb_}", wod)
                        if fc == 0:
                            wait("tensor", "V", v_evall)
                        gwait("tensor", gd.get(f"sbank{fc % 2}"))
                        for dc in range(DC):
                            mm(ps[b][:, :], wofc[wb_][:, dc * 128:(dc + 1) * 128],
                               attno[:, dc * 512:(dc + 1) * 512], dc == 0, dc == DC - 1)
                        t_wo = tinc()
                        if fc >= 1:
                            ln2_stats(fc - 1)
                        wait("vector", "T", t_wo)
                        gwait("vector", gd.get(f"x1t{fc % 2}"))
                        if fc == 0:
                            wores = []
                            gwait("vector", gd.get("rlB0"))
                        wores.append(vop("tensor_tensor", rlB[0][:, :], ps[b][:, :],
                                         xtt[:, fc * 512:(fc + 1) * 512], ALU.add))
                        v_x1 = vop("tensor_tensor", x1t[fc % 2][:, :], rlB[0][:, :],
                                   xnn[:, fc * 512:(fc + 1) * 512], ALU.add, inc=True)
                        gd[f"sbank{fc % 2}"] = ("V", v_x1)
                        gwait("vector", gd.get("qraw0"))
                        vop("tensor_copy", qraw[0][:, :], x1t[fc % 2][:, :])
                        gwait("vector", gd.get("qsq"))
                        v_x1b = vop("tensor_tensor", qsq[:, :], qraw[0][:, :], qraw[0][:, :],
                                    ALU.mult, inc=True)
                        v_x1b_of[fc] = v_x1b
                        wait("sync", "V", v_x1)
                        spd = dma("sync", x1s[fc, :, qc * 512:(qc + 1) * 512],
                                  x1t[fc % 2][:, :], f"SP{fc % 2}")
                        gd[f"x1t{fc % 2}"] = (f"SP{fc % 2}", spd)
                        gd[f"wofc{wb_}"] = ("T", t_wo)
                        if fc == FC - 1:
                            gd["rlB0"] = ("V", v_x1)
                        if qc < NQC - 1:
                            if fc == 0:
                                preload[0] = []
                            wait("sync", "V", wores[fc])
                            preload[0].append(dma(
                                "sync", xtt[:, fc * 512:(fc + 1) * 512],
                                xq[:, (qc + 1) * 512:(qc + 2) * 512].rearrange(
                                    "(dc p) t -> p dc t", p=128)[:, fc, :], f"X{fc}"))
                        else:
                            preload[0] = None
                        if fc >= 2 and xstat_cls:
                            xstat_cls.pop(0)()
                    while xstat_cls:
                        xstat_cls.pop(0)()
                    ln2_stats(FC - 1)
                    t_ln2 = cnt["T"]
                    wait("vector", "T", t_ln2)
                    gwait("vector", gd.get("rows1"))
                    vop("tensor_scalar", row(R_MEAN), ps[PS_L0][0:1, :], 1.0 / D, None, ALU.mult)
                    vop("tensor_scalar", row(R_VAR), ps[PS_L1][0:1, :], 1.0 / D, None, ALU.mult)
                    vop("tensor_tensor", row(R_T1), row(R_MEAN), row(R_MEAN), ALU.mult)
                    v_r2 = vop("tensor_tensor", row(R_VAR), row(R_VAR), row(R_T1),
                               ALU.subtract, inc=True)
                    gd["lbank0"] = ("V", v_r2)
                    gd["lbank1"] = ("V", v_r2)
                    wait("scalar", "V", v_r2)
                    act(row(R_LN), row(R_VAR), AF.Ln, bias=epsb[0:1, 0:1], scale=1.0)
                    s_i2 = act(row(R_INV), row(R_LN), AF.Exp, scale=-0.5, inc=True)
                    wait("vector", "S", s_i2)
                    v_m2r = vop("tensor_tensor", row(R_MEAN), row(R_MEAN), row(R_INV),
                                ALU.mult, inc=True)
                    i2s, m2s = new_slot(), new_slot()
                    inv2_slot[qc] = i2s
                    minv2_slot[qc] = m2s
                    wait("sync", "V", v_m2r)
                    dma("sync", rbc[i2s:i2s + 1, :], row(R_INV), "RW")
                    ld2 = dma("sync", rbc[m2s:m2s + 1, :], row(R_MEAN), "RW")
                    wait("sync", "RW", ld2)
                    gd["rows1"] = [("S", s_i2), ("RW", ld2)]
                    wores_prev = wores
                    gd["xnn"] = ("V", cnt["V"])
                    gd["xnn_sq"] = ("T", cnt["T"])

                barrier()
                flush(blk)

        with ExitStack() as SB:
            fuse = SB.enter_context(nc.sbuf_tensor("fuse", [128, NICG * 1024], BF16))
            xn2 = SB.enter_context(nc.sbuf_tensor("xn2", [128, DC * 1024], BF16))
            gbuf = [SB.enter_context(nc.sbuf_tensor(f"gbuf{i}", [128, DC * 256], U8))
                    for i in range(2)]
            ubuf = [SB.enter_context(nc.sbuf_tensor(f"ubuf{i}", [128, DC * 256], U8))
                    for i in range(2)]
            dbuf = [SB.enter_context(nc.sbuf_tensor(f"dbuf{i}", [128, NICG * 128], U8))
                    for i in range(2)]
            inv2B = [SB.enter_context(nc.sbuf_tensor(f"inv2B{i}", [128, 512], F32))
                     for i in range(2)]
            minv2B = [SB.enter_context(nc.sbuf_tensor(f"minv2B{i}", [128, 512], F32))
                      for i in range(2)]
            x1f = [SB.enter_context(nc.sbuf_tensor(f"x1f{i}", [128, 512], F32))
                   for i in range(4)]
            x1g = [SB.enter_context(nc.sbuf_tensor(f"x1g{i}", [128, 512], F32))
                   for i in range(2)]
            fft = SB.enter_context(nc.sbuf_tensor("fft", [128, 512], F32))
            gact = [SB.enter_context(nc.sbuf_tensor(f"gact{i}", [128, 512], BF16))
                    for i in range(2)]
            outt = [SB.enter_context(nc.sbuf_tensor(f"outt{i}", [128, 512], F32))
                    for i in range(2)]

            def xn2_prep(hf):
                va = {0: {}, 1: {}}
                i2d = {}
                cls = []
                for tc2 in range(2):
                    qc = 2 * hf + tc2

                    def mk_i2(tc2=tc2, qc=qc):
                        def fn():
                            gwait("sync", gd.get(f"inv2B{tc2}"))
                            dma("sync", inv2B[tc2][:, :],
                                rbc[inv2_slot[qc]:inv2_slot[qc] + 1, :]
                                .broadcast_to([128, 512]), f"I2{tc2}")
                            i2d[tc2] = dma(
                                "sync", minv2B[tc2][:, :],
                                rbc[minv2_slot[qc]:minv2_slot[qc] + 1, :]
                                .broadcast_to([128, 512]), f"I2{tc2}")
                        return fn
                    cls.append(mk_i2())
                    for fc in range(FC):
                        def mk_tile(tc2=tc2, qc=qc, fc=fc):
                            def fn():
                                sl4 = (tc2 * FC + fc) % 4
                                gwait("sync", gd.get(f"x1f{sl4}"))
                                xfd = dma("sync", x1f[sl4][:, :],
                                          x1s[fc, :, qc * 512:(qc + 1) * 512], f"XF{sl4}")
                                wait("vector", f"XF{sl4}", xfd)
                                wait("vector", f"I2{tc2}", i2d[tc2])
                                gwait("vector", gd.get("xn2w"))
                                vop("tensor_tensor", fft[:, :], x1f[sl4][:, :],
                                    inv2B[tc2][:, :], ALU.mult)
                                v_a = vop("tensor_tensor",
                                          xn2[:, fc * 1024 + tc2 * 512:
                                              fc * 1024 + (tc2 + 1) * 512],
                                          fft[:, :], minv2B[tc2][:, :], ALU.subtract,
                                          inc=True)
                                gd[f"x1f{sl4}"] = ("V", v_a)
                                va[tc2][fc] = v_a
                                if fc == FC - 1:
                                    gd[f"inv2B{tc2}"] = ("V", v_a)
                            return fn
                        cls.append(mk_tile())
                return cls, va

            cls0, va0 = xn2_prep(0)
            for fn in cls0:
                fn()
            cls1, va1 = xn2_prep(1)
            dwd_h = {}
            for hf in range(2):
                va = va0 if hf == 0 else va1
                if hf == 1:
                    while cls1:
                        cls1.pop(0)()
                GB = [PS_S0, PS_S1]
                UB = [PS_C, PS_D]
                alt = 0
                for ich in range(ICH):
                    wbuf = ich % 2
                    gwait("gpsimd", gd.get(f"gbuf{wbuf}"))
                    dma("gpsimd", gbuf[wbuf].bitcast(FP8)[:, :],
                        gw[:, ich * 256:(ich + 1) * 256].rearrange("(dc p) f -> p dc f", p=128),
                        f"GU{wbuf}")
                    gud = dma("gpsimd", ubuf[wbuf].bitcast(FP8)[:, :],
                              uw[:, ich * 256:(ich + 1) * 256].rearrange("(dc p) f -> p dc f", p=128),
                              f"GU{wbuf}")
                    if ich in (ICH - 2, ICH - 1):
                        pre = ich - (ICH - 2)
                        gwait("gpsimd", gd.get(f"dbuf{pre}"))
                        dwd_h[pre] = dma(
                            "gpsimd", dbuf[pre].bitcast(FP8)[:, :],
                            dw[:, pre * 128:(pre + 1) * 128]
                            .rearrange("(icg p) f -> p icg f", p=128), f"DW{pre}")
                    for icsub in range(2):
                        icg = ich * 2 + icsub
                        for tc2 in range(2):
                            gb = GB[alt % 2]
                            ub = UB[alt % 2]
                            wait("tensor", f"GU{wbuf}", gud)
                            gwait("tensor", gd.get(f"gub{alt % 2}"))
                            if ich == 0 and icsub == 0:
                                wait("tensor", "V", va[tc2][FC - 1])
                            for dc in range(DC):
                                mm(ps[gb][:, :],
                                   gbuf[wbuf].bitcast(FP8)[:, dc * 256 + icsub * 128: dc * 256 + (icsub + 1) * 128],
                                   xn2[:, dc * 1024 + tc2 * 512: dc * 1024 + (tc2 + 1) * 512],
                                   dc == 0, dc == DC - 1)
                            t_g = tinc()
                            for dc in range(DC):
                                mm(ps[ub][:, :],
                                   ubuf[wbuf].bitcast(FP8)[:, dc * 256 + icsub * 128: dc * 256 + (icsub + 1) * 128],
                                   xn2[:, dc * 1024 + tc2 * 512: dc * 1024 + (tc2 + 1) * 512],
                                   dc == 0, dc == DC - 1)
                            t_u = tinc()
                            wait("scalar", "T", t_g)
                            gwait("scalar", gd.get(f"gact{alt % 2}"))
                            s_g = act(gact[alt % 2][:, :], ps[gb][:, :], AF.Silu,
                                      scale=gs_sb[:, icg:icg + 1], inc=True)
                            wait("vector", "T", t_u)
                            wait("vector", "S", s_g)
                            v_f = vop("scalar_tensor_tensor",
                                      fuse[:, icg * 1024 + tc2 * 512: icg * 1024 + (tc2 + 1) * 512],
                                      ps[ub][:, :], us_sb[:, icg:icg + 1], gact[alt % 2][:, :],
                                      ALU.mult, ALU.mult, inc=True)
                            gd[f"gub{alt % 2}"] = ("V", v_f)
                            gd[f"gact{alt % 2}"] = ("V", v_f)
                            alt += 1
                    gd[f"gbuf{wbuf}"] = ("T", cnt["T"])
                    v_fuse = cnt["V"]
                gd["xn2w"] = ("T", cnt["T"])

                DB = [PS_L0, PS_L1]
                for fc in range(FC):
                    wbuf = fc % 2
                    dwd = dwd_h[fc]
                    for tc2 in range(2):
                        qc = 2 * hf + tc2
                        gwait("sync", gd.get(f"x1g{tc2}"))
                        xgd = dma("sync", x1g[tc2][:, :],
                                  x1s[fc, :, qc * 512:(qc + 1) * 512], f"XG{tc2}")
                        gd[f"_xg{tc2}"] = (f"XG{tc2}", xgd)
                    wait("tensor", f"DW{wbuf}", dwd)
                    wait("tensor", "V", v_fuse)
                    gwait("tensor", gd.get("dbank0"))
                    gwait("tensor", gd.get("dbank1"))
                    for icg in range(NICG):
                        for tc2 in range(2):
                            mm(ps[DB[tc2]][:, :],
                               dbuf[wbuf].bitcast(FP8)[:, icg * 128:(icg + 1) * 128],
                               fuse[:, icg * 1024 + tc2 * 512: icg * 1024 + (tc2 + 1) * 512],
                               icg == 0, icg == NICG - 1)
                    t_dn = tinc()
                    gd[f"dbuf{wbuf}"] = ("T", t_dn)
                    if fc + 2 < FC:
                        gwait("gpsimd", gd.get(f"dbuf{wbuf}"))
                        dwd_h[fc + 2] = dma(
                            "gpsimd", dbuf[wbuf].bitcast(FP8)[:, :],
                            dw[:, (fc + 2) * 128:(fc + 3) * 128]
                            .rearrange("(icg p) f -> p icg f", p=128), f"DW{wbuf}")
                    for tc2 in range(2):
                        wait("vector", "T", t_dn)
                        gwait("vector", gd.get(f"_xg{tc2}"))
                        gwait("vector", gd.get(f"outt{tc2}"))
                        v_o = vop("scalar_tensor_tensor", outt[tc2][:, :],
                                  ps[DB[tc2]][:, :], ds_sb[:, fc:fc + 1], x1g[tc2][:, :],
                                  ALU.mult, ALU.add, inc=True)
                        gd[f"x1g{tc2}"] = ("V", v_o)
                        gd[f"dbank{tc2}"] = ("V", v_o)
                        wait("sync", "V", v_o)
                        od = dma("sync", out[fc * 128:(fc + 1) * 128,
                                           hf * 1024 + tc2 * 512: hf * 1024 + (tc2 + 1) * 512],
                                 outt[tc2][:, :], f"OUT{tc2}")
                        gd[f"outt{tc2}"] = (f"OUT{tc2}", od)
                    if hf == 0:
                        for _ in range(3):
                            if cls1:
                                cls1.pop(0)()

            for e, ss in OWN.items():
                for sn in ss:
                    wait(e, sn, cnt[sn])
            flush(blk)

    return nc


def _host_masks(parity):
    m8 = np.ones((8, 128, 512), np.float32)
    pidx = np.arange(128)[:, None]
    qidx = np.arange(128)[None, :]
    tri = np.where(qidx >= pidx, 1.0, 0.0).astype(np.float32)
    for m in range(8):
        o = m - parity
        if o < 0:
            continue
        for j in range(4):
            d = 2 * j - o
            if d < 0:
                m8[m][:, j * 128:(j + 1) * 128] = 0.0
            elif d == 0:
                m8[m][:, j * 128:(j + 1) * 128] = tri
    return m8


def kernel(L0_attn_x, L0_attn_Wq, L0_attn_Wk, L0_attn_Wv, L0_attn_Wo,
           L0_attn_q_gamma, L0_attn_k_gamma, L0_ffn_gate_w, L0_ffn_up_w,
           L0_ffn_down_w, L0_ffn_gate_scale, L0_ffn_up_scale, L0_ffn_down_scale):
    x = np.ascontiguousarray(np.asarray(L0_attn_x), dtype=np.float32)
    wq_ = np.ascontiguousarray(np.asarray(L0_attn_Wq), dtype=np.float32)
    wk_ = np.ascontiguousarray(np.asarray(L0_attn_Wk), dtype=np.float32)
    wv_ = np.ascontiguousarray(np.asarray(L0_attn_Wv), dtype=np.float32)
    wo_ = np.ascontiguousarray(np.asarray(L0_attn_Wo), dtype=np.float32)
    qg_ = np.ascontiguousarray(np.asarray(L0_attn_q_gamma), dtype=np.float32).reshape(128, 1)
    kg_ = np.ascontiguousarray(np.asarray(L0_attn_k_gamma), dtype=np.float32).reshape(128, 1)
    gw_ = np.ascontiguousarray(np.asarray(L0_ffn_gate_w), dtype=np.float32)
    uw_ = np.ascontiguousarray(np.asarray(L0_ffn_up_w), dtype=np.float32)
    dw_ = np.ascontiguousarray(np.asarray(L0_ffn_down_w), dtype=np.float32)
    gs_ = np.ascontiguousarray(np.asarray(L0_ffn_gate_scale), dtype=np.float32).reshape(NICG, 128).T.copy()
    us_ = np.ascontiguousarray(np.asarray(L0_ffn_up_scale), dtype=np.float32).reshape(NICG, 128).T.copy()
    ds_ = np.ascontiguousarray(np.asarray(L0_ffn_down_scale), dtype=np.float32).reshape(FC, 128).T.copy()

    xT_b = [np.ascontiguousarray(x[b].T) for b in range(4)]
    masks = [_host_masks(p) for p in range(2)]
    own_cols = [np.concatenate([np.arange(g * 128, (g + 1) * 128)
                                for g in range(p, 32, 2)]) for p in range(2)]

    nc = _build()
    in_maps = []
    for c in range(8):
        b, p = c // 2, c % 2
        in_maps.append({
            "xT": xT_b[b],
            "xq": np.ascontiguousarray(xT_b[b][:, own_cols[p]]),
            "wq": wq_, "wk": wk_, "wv": wv_, "wo": wo_,
            "gw": gw_, "uw": uw_, "dw": dw_,
            "qg": qg_, "kg": kg_,
            "gs": gs_, "us": us_, "dsc": ds_,
            "mask8": masks[p],
        })
    res = run_bass_kernel_spmd(nc, in_maps, core_ids=list(range(8)))
    outp = np.empty((4, S, D), np.float32)
    for c in range(8):
        b, p = c // 2, c % 2
        o = res.results[c]["out"]
        oT = np.ascontiguousarray(o.T)
        for j, g in enumerate(range(p, 32, 2)):
            outp[b, g * 128:(g + 1) * 128, :] = oT[j * 128:(j + 1) * 128, :]
    return outp
